# version 07
# baseline (speedup 1.0000x reference)
import numpy as np
import ml_dtypes
from contextlib import ExitStack

import concourse.bass as bass
import concourse.mybir as mybir
from concourse.bass_utils import run_bass_kernel_spmd

F32 = mybir.dt.float32
BF16 = mybir.dt.bfloat16
I32 = mybir.dt.int32
U8 = mybir.dt.uint8
AF = mybir.ActivationFunctionType
ALU = mybir.AluOpType
AX = mybir.AxisListType

NCORES = 8
D = 4096
NH = 32
DH = 128
NJ = D // 128
EPS = 1e-5
QSCALE = 1.0 / float(np.sqrt(DH))
MASK_NEG = -30000.0

ENGS = ("sy", "gp", "t", "v", "s")
CHAIN_ENGS = ()
DECLARED_SELF = ("v", "s")


class Prog:

    def __init__(self):
        self.ev = []
        self.last_of = {e: None for e in ENGS}
        self.last_stream = {}

    def add(self, eng, fn, deps=(), tag="", stream=None):
        i = len(self.ev)
        deps = [d for d in deps if d is not None]
        if stream is not None:
            prev = self.last_stream.get((eng, stream))
            if prev is not None:
                deps.append(prev)
            self.last_stream[(eng, stream)] = i
        self.ev.append(dict(i=i, eng=eng, fn=fn, deps=deps, tag=tag,
                            stream=stream, need=False, val=None, sem=None))
        self.last_of[eng] = i
        return i

    def barrier(self, tag="bar"):
        lasts = [self.last_of[e] for e in ENGS if self.last_of[e] is not None]
        lasts += [v for v in self.last_stream.values()]
        out = {}
        for e in ENGS:
            out[e] = self.add(e, None, deps=list(lasts), tag=tag)
        return out

    def emit_fns(self, get_sem):
        ev = self.ev
        for e_ in ev:
            for d in e_["deps"]:
                ev[d]["need"] = True
        counters = {}
        for e_ in ev:
            eng = e_["eng"]
            if eng in ("sy", "gp"):
                if e_["fn"] is not None:
                    sname = f"D:{eng}:{e_['stream'] or 'def'}"
                    counters[sname] = counters.get(sname, 0) + 16
                    e_["sem"], e_["val"] = sname, counters[sname]
                else:
                    sname = eng + "m"
                    counters[sname] = counters.get(sname, 0) + 1
                    e_["sem"], e_["val"] = sname, counters[sname]
            elif eng in CHAIN_ENGS or e_["need"] or e_["fn"] is None:
                counters[eng] = counters.get(eng, 0) + 1
                e_["sem"], e_["val"] = eng, counters[eng]
        per_eng = {e: [] for e in ENGS}
        for e_ in ev:
            per_eng[e_["eng"]].append(e_)

        def run_engine(handle, eng):
            satisfied = {}
            prev_val = None
            for e_ in per_eng[eng]:
                waits = {}
                for d in e_["deps"]:
                    p = ev[d]
                    if p["val"] is None:
                        continue
                    if p["eng"] == eng and not p["sem"].startswith("D:"):
                        if eng not in DECLARED_SELF:
                            continue
                    if (p["eng"] == eng and e_["fn"] is not None
                            and e_["sem"] == p["sem"]):
                        pass
                    sname = p["sem"]
                    if satisfied.get(sname, 0) >= p["val"]:
                        continue
                    waits[sname] = max(waits.get(sname, 0), p["val"])
                if eng in CHAIN_ENGS and prev_val is not None:
                    waits[eng] = max(waits.get(eng, 0), prev_val)
                for se, v in sorted(waits.items()):
                    handle.wait_ge(get_sem(se), v)
                    satisfied[se] = v
                if e_["fn"] is not None:
                    inst = e_["fn"](handle)
                    if e_["val"] is not None:
                        inst.then_inc(get_sem(e_["sem"]),
                                      16 if e_["sem"].startswith("D:") else 1)
                elif e_["val"] is not None:
                    handle.nop().then_inc(get_sem(e_["sem"]), 1)
                if e_["val"] is not None and e_["sem"] == eng:
                    prev_val = e_["val"]
        return run_engine


def build_nc(T=8192, I_FFN=11008):
    NT = T // 128
    NPOS = NT // NCORES
    NA = NPOS // 2
    OWN = NPOS * 128
    NI = I_FFN // 128
    GTP = min(512, OWN)
    NPASS = OWN // GTP
    NHALF = max(1, OWN // 512)
    HT = OWN // NHALF

    def nch(p):
        return (p + 1) if p < NA else (NT // 8 - (p - NA))

    nc = bass.Bass("TRN2", target_bir_lowering=False)

    d_x = nc.dram_tensor("x", [T, D], F32, kind="ExternalInput").ap()
    d_xq = nc.dram_tensor("xq", [NPOS, 128, D], F32, kind="ExternalInput").ap()
    d_wq = nc.dram_tensor("wq", [D, D], F32, kind="ExternalInput").ap()
    d_wk = nc.dram_tensor("wk", [D, DH], F32, kind="ExternalInput").ap()
    d_wv = nc.dram_tensor("wv", [D, DH], F32, kind="ExternalInput").ap()
    d_wo = nc.dram_tensor("wo", [D, D], F32, kind="ExternalInput").ap()
    d_wg = nc.dram_tensor("wg", [I_FFN, D], I32, kind="ExternalInput").ap()
    d_wu = nc.dram_tensor("wu", [I_FFN, D], I32, kind="ExternalInput").ap()
    d_wd = nc.dram_tensor("wd", [D, I_FFN], I32, kind="ExternalInput").ap()
    d_sg = nc.dram_tensor("sg", [128, NI], F32, kind="ExternalInput").ap()
    d_su = nc.dram_tensor("su", [128, NI], F32, kind="ExternalInput").ap()
    d_sd = nc.dram_tensor("sd", [128, NJ], F32, kind="ExternalInput").ap()
    d_eye16 = nc.dram_tensor("eye16", [128, 128], BF16, kind="ExternalInput").ap()
    d_eye32 = nc.dram_tensor("eye32", [128, 128], F32, kind="ExternalInput").ap()
    d_mask = nc.dram_tensor("mask", [NPOS, 128, 1024], F32, kind="ExternalInput").ap()
    d_out = nc.dram_tensor("out", [NPOS, 128, D], F32, kind="ExternalOutput").ap()
    d_wdt = nc.dram_tensor("wdt_s", [NI, 128, D], BF16).ap()
    d_hsp = (nc.dram_tensor("h_sp", [NI, 128, OWN - GTP], BF16).ap()
             if NPASS > 1 else None)
    d_ysp = nc.dram_tensor("y_sp", [NPOS, 128, D], F32).ap()
    d_asp = nc.dram_tensor("attn_sp", [NJ, 128, OWN], F32).ap()

    P = Prog()
    ARENA = 204800

    with ExitStack() as stk:
        en = stk.enter_context
        AR = en(nc.sbuf_tensor("AR", [128, ARENA], U8))
        arf = AR[:, :].bitcast(F32)
        arb = AR[:, :].bitcast(BF16)

        def fbuf(off, cols):
            assert off % 4 == 0 and off + 4 * cols <= ARENA, (off, cols)
            return arf[:, off // 4: off // 4 + cols]

        def bbuf(off, cols):
            assert off % 2 == 0 and off + 2 * cols <= ARENA, (off, cols)
            return arb[:, off // 2: off // 2 + cols]

        eye16 = bbuf(0, 128)
        eye32 = fbuf(512, 128)
        sg_sb = fbuf(1024, NI)
        su_sb = fbuf(1536, NI)
        sd_sb = fbuf(2048, NJ)
        ST = 2560

        def stat(slot):
            return fbuf(ST + 4 * slot, 1)
        wk_sb = bbuf(3072, NJ * 128)
        wv_sb = bbuf(11264, NJ * 128)
        BUFO = 19456
        BUF = bbuf(BUFO, NH * OWN)
        PH = BUFO + NH * OWN * 2
        KT = bbuf(PH, NT * 128)
        VST = 132
        VSB = bbuf(PH + NT * 256, NT * VST)
        PA = PH + NT * 256 + ((NT * VST * 2 + 511) // 512) * 512
        xt = [fbuf(PA + i * 16384, D) for i in range(2)]
        xn = [bbuf(PA + 32768 + i * 8192, D) for i in range(2)]
        xnT = [bbuf(PA + 49152 + i * 8192, D) for i in range(2)]
        wq_sb = [bbuf(PA + 16384 + i * 8192, 8 * 512) for i in range(2)]
        xnqT = bbuf(PA + 49152, NJ * HT)
        mask_sb = fbuf(PA, NPOS * 1024)
        p_sb = [bbuf(PA + 32768 + i * 2048, 1024) for i in range(2)]
        pt_sb = [bbuf(PA + 36864 + i * 2048, 1024) for i in range(2)]
        o_sbs = [bbuf(PA + 40960, D), bbuf(PA + 51712, D)]
        o_acc = [fbuf(PA + 49152 + i * 640, VST) for i in range(4)]
        wo_sb = [bbuf(PH + i * 8192, NJ * 128) for i in range(2)]
        att_scr = [fbuf(PH + 16384 + i * 2048, 512) for i in range(2)]
        attnb = fbuf(PA, D)
        xqt = fbuf(PA + 16384, D)
        y_sb = fbuf(PA + 32768, D)
        yn_sb = bbuf(PA + 49152, D)
        h_all = bbuf(PH, NI * GTP)
        F1B = PH + ((NI * GTP * 2 + 511) // 512) * 512
        wbf_in = [bbuf(F1B + b * 2048, 1024) for b in range(4)]
        wT = [bbuf(F1B + 8192 + b * 2048, 1024) for b in range(4)]
        wdtcg = [bbuf(F1B + 16384 + b * 2048, 1024) for b in range(2)]
        h_i = [bbuf(F1B + 20480 + b * ((OWN * 2 + 511) // 512) * 512, OWN)
               for b in range(2)]
        HIE = F1B + 20480 + 2 * ((OWN * 2 + 511) // 512) * 512
        silu_scr = [fbuf(HIE + b * 2048, 512) for b in range(2)]
        assert HIE + 4096 <= ARENA, (HIE, ARENA)
        strip = [bbuf(BUFO + i * NI * 256, NI * 128) for i in range(2)]
        SS = BUFO + 2 * NI * 256
        dscr = [fbuf(SS + i * 2048, GTP) for i in range(2)]
        yscr = [fbuf(SS + 4096 + i * 512, 128) for i in range(2)]
        oscr = [fbuf(SS + 5120 + i * 512, 128) for i in range(2)]
        assert SS + 6144 <= BUFO + NH * OWN * 2

        ps_a = en(nc.psum_tensor("ps_a", [128, 1024], F32))
        ps_b = en(nc.psum_tensor("ps_b", [128, 1024], F32))
        pf_a = en(nc.psum_tensor("pf_a", [128, 512], F32))
        pf_b = en(nc.psum_tensor("pf_b", [128, 512], F32))
        pb_t = [en(nc.psum_tensor(f"pb{i}", [128, 1024], BF16)) for i in range(2)]
        sem_cache = {}

        def get_sem(name):
            if name not in sem_cache:
                sem_cache[name] = en(nc.semaphore(
                    "sem_" + name.replace(":", "_")))
            return sem_cache[name]

        def dma(q, out_ap, in_ap, deps=(), tag="", stream=None):
            return P.add(q, lambda e, o=out_ap, i=in_ap: e.dma_start(o, i),
                         deps, tag, stream=(stream or tag or "def"))

        add = P.add

        ld_eye16 = dma("sy", eye16, d_eye16, tag="eye16", stream="c")
        ld_eye32 = dma("sy", eye32, d_eye32, tag="eye32", stream="c")
        ld_sg = dma("sy", sg_sb, d_sg, stream="c")
        ld_su = dma("sy", su_sb, d_su, stream="c")
        ld_sd = dma("sy", sd_sb, d_sd, stream="c")
        ld_wk = dma("gp", wk_sb, d_wk.rearrange("(j p) c -> p j c", p=128), stream="gc")
        ld_wv = dma("gp", wv_sb, d_wv.rearrange("(j p) c -> p j c", p=128), stream="gc")
        ld_ones = None
        eps_ap = stat(60)
        ld_eps = add("v", lambda e: e.memset(eps_ap, float(EPS)), [], "eps")

        def ln_chain(par, src, dst_bf, deps_in):
            s_sum, s_sq = stat(0 + par), stat(2 + par)
            s_mean, s_var = stat(4 + par), stat(6 + par)
            s_sd, s_rstd = stat(8 + par), stat(10 + par)
            s_negmr, s_m2 = stat(12 + par), stat(14 + par)
            e1 = add("v", lambda e: e.reduce_sum(s_sum, src, axis=AX.X),
                     deps_in, "ln_sum")
            e2 = add("s", lambda e: e.activation(dst_bf, src, AF.Square,
                                                 accum_out=s_sq),
                     deps_in, "ln_sq")
            e3 = add("v", lambda e: e.tensor_scalar_mul(s_mean, s_sum, 1.0 / D),
                     [e1], "ln_mean")
            e4 = add("v", lambda e: e.tensor_mul(s_m2, s_mean, s_mean), [e3])
            e5 = add("v", lambda e: e.scalar_tensor_tensor(
                s_var, s_sq, 1.0 / D, s_m2, ALU.mult, ALU.subtract), [e2, e4])
            e6 = add("s", lambda e: e.activation(s_sd, s_var, AF.Sqrt,
                                                 bias=eps_ap), [e5, ld_eps])
            e7 = add("v", lambda e: e.reciprocal(s_rstd, s_sd), [e6])
            e8 = add("v", lambda e: e.scalar_tensor_tensor(
                s_negmr, s_mean, -1.0, s_rstd, ALU.mult, ALU.mult), [e7, e3])
            e9 = add("s", lambda e: e.activation(dst_bf, src, AF.Identity,
                                                 bias=s_negmr, scale=s_rstd),
                     [e8], "ln_apply")
            return e9

        pb_rel = {}

        def transpose_tile(src_bf, dep_src, dst_fn, tag=""):
            evs = []
            for jg in range(NJ // 8):
                pbk = pb_t[jg % 2]
                rel = pb_rel.get(jg % 2)
                trs = []
                for s in range(8):
                    j = jg * 8 + s
                    trs.append(add("t", lambda e, j=j, s=s, p=pbk:
                                   e.transpose(p[:, s * 128:(s + 1) * 128],
                                               src_bf[:, j * 128:(j + 1) * 128],
                                               eye16),
                                   [dep_src, ld_eye16, rel], tag + "tr"))
                dst_ap, xdeps = dst_fn(jg)
                ev = add("v", lambda e, p=pbk, da=dst_ap: e.tensor_copy(da, p[:, :]),
                         trs + list(xdeps), tag + "ev")
                pb_rel[jg % 2] = ev
                evs.append(ev)
            return evs

        ld_ones = add("v", lambda e: e.memset(
            VSB[:, :].rearrange("q (t c) -> q t c", c=VST)[:, :, 128:129], 1.0),
            [], "vones")
        xt_rel = {}
        xn_rel = {}
        xnT_rel = {}
        kv_rel = {}
        last_kv = [None, None]
        for i in range(NT):
            par = i % 2
            ld = dma("sy", xt[par], d_x[i * 128:(i + 1) * 128, :],
                     deps=[xt_rel.get(par)], tag=f"x{i}", stream=f"x{par}")
            ap9 = ln_chain(par, xt[par], xn[par], [ld, xn_rel.get(par)])
            xt_rel[par] = ap9

            def dst_fn(jg, par=par):
                return xnT[par][:, jg * 1024:(jg + 1) * 1024], \
                    ([xnT_rel.get(par)] if xnT_rel.get(par) is not None else [])
            evacs = transpose_tile(xn[par], ap9, dst_fn, tag=f"a{i}")
            xn_rel[par] = evacs[-1]
            kmm = vmm = None
            for j in range(NJ):
                dep = [evacs[j // 8], ld_wk]
                if j == 0:
                    dep.append(kv_rel.get("k"))
                kmm = add("t", lambda e, j=j, par=par: e.matmul(
                    pf_a[:, 0:128], wk_sb[:, j * 128:(j + 1) * 128],
                    xnT[par][:, j * 128:(j + 1) * 128],
                    start=(j == 0), stop=(j == NJ - 1)), dep, "kmm")
            kev = add("v", lambda e, i=i: e.tensor_copy(
                KT[:, i * 128:(i + 1) * 128], pf_a[:, 0:128]), [kmm], "kev")
            kv_rel["k"] = kev
            for j in range(NJ):
                dep = [evacs[j // 8], ld_wv]
                if j == 0:
                    dep.append(kv_rel.get("v"))
                vmm = add("t", lambda e, j=j, par=par: e.matmul(
                    pf_b[:, 0:128], xnT[par][:, j * 128:(j + 1) * 128],
                    wv_sb[:, j * 128:(j + 1) * 128],
                    start=(j == 0), stop=(j == NJ - 1)), dep, "vmm")
            vev = add("v", lambda e, i=i: e.tensor_copy(
                VSB[:, i * VST:i * VST + 128], pf_b[:, 0:128]), [vmm], "vev")
            kv_rel["v"] = vev
            xnT_rel[par] = vmm
            last_kv = [kev, vev]

        P.barrier("endP1a")

        qt_last = None
        psq = [ps_a[:, 0:HT], ps_a[:, 512:512 + HT],
               ps_b[:, 0:HT], ps_b[:, 512:512 + HT]]
        psq_rel = [None] * 4
        wq_rel = {}
        xq_rel = None
        for half in range(NHALF):
            ev_groups = {jg: [] for jg in range(4)}
            prev_ln = None
            for ti in range(HT // 128):
                p = half * (HT // 128) + ti
                ld = dma("sy", xt[0], d_xq[p], deps=[xq_rel], tag=f"xq{p}", stream="xq")
                ap9 = ln_chain(p % 2, xt[0], xn[p % 2], [ld, prev_ln])
                xq_rel = ap9

                def dst_fn(jg, ti=ti, half=half):
                    da = xnqT[:, :].rearrange("q (j c) -> q j c", c=HT)[
                        :, jg * 8:(jg + 1) * 8, ti * 128:(ti + 1) * 128]
                    xd = [qt_last] if (half > 0 and ti == 0) else []
                    return da, xd
                evacs = transpose_tile(xn[p % 2], ap9, dst_fn, tag=f"b{p}")
                prev_ln = evacs[-1]
                for jg in range(4):
                    ev_groups[jg].append(evacs[jg])
            for mb in range(8):
                last_mm = None
                for jg in range(4):
                    wpar = (mb * 4 + jg) % 2
                    src = d_wq.rearrange("(j p) c -> p j c", p=128)[
                        :, jg * 8:(jg + 1) * 8, mb * 512:(mb + 1) * 512]
                    ldw = dma("gp", wq_sb[wpar], src,
                              deps=[wq_rel.get(wpar)], tag="wq",
                              stream=f"wq{wpar}")
                    for jj in range(8):
                        j = jg * 8 + jj
                        for m in range(4):
                            dep = [ldw] + ev_groups[jg]
                            if j == 0:
                                dep.append(psq_rel[m])
                            st_ = (j == 0)
                            sp_ = (j == NJ - 1)
                            last_mm = add("t", lambda e, jj=jj, m=m, wpar=wpar,
                                          j=j, st_=st_, sp_=sp_: e.matmul(
                                psq[m],
                                wq_sb[wpar][:, jj * 512 + m * 128:
                                            jj * 512 + (m + 1) * 128],
                                xnqT[:, j * HT: (j + 1) * HT],
                                start=st_, stop=sp_), dep, "qmm")
                    wq_rel[wpar] = last_mm
                for m in range(4):
                    h = mb * 4 + m
                    ev = add("v", lambda e, h=h, m=m, half=half: e.tensor_scalar_mul(
                        BUF[:, h * OWN + half * HT: h * OWN + (half + 1) * HT],
                        psq[m], QSCALE), [last_mm], "qev")
                    psq_rel[m] = ev
                    qt_last = ev

        P.barrier("endP2b_pre")

        ld_mask = dma("sy", mask_sb, d_mask.rearrange("p q c -> q p c"),
                      tag="mask", stream="mask")
        ps_s = [ps_a, ps_b]
        o_ps = [pf_a, pf_b]
        NPH = 4
        mneg = [stat(16 + i) for i in range(NPH)]
        mpos = [stat(20 + i) for i in range(NPH)]
        alpha = [stat(24 + i) for i in range(4)]
        recip = [stat(36 + i) for i in range(NPH)]
        mtmp = [stat(40 + i) for i in range(NPH)]
        s_rel = {}
        p_rel = {}
        pts_rel = {}
        avps_rel = {}
        ofin_rel = {}
        eev_last = {}
        mneg_last = {}
        alpha_rel = {}
        st_p2a = {}

        recs = []
        hidx = 0
        pend_ot = []
        for p in range(NPOS):
            NCH = nch(p)
            for h in range(NH):
                for n in range(NCH):
                    recs.append(dict(kind="c", p=p, h=h, n=n, nch=NCH,
                                     ph=hidx % NPH))
                hidx += 1
            for jg in range(4):
                pend_ot.append((len(recs) - 1, p, jg))
        out_recs = []
        qi = 0
        src_pos = {}
        for i, r in enumerate(recs):
            out_recs.append(r)
            src_pos[i] = len(out_recs) - 1
            while (qi < len(pend_ot) and pend_ot[qi][0] <= i
                   and len(out_recs) >= src_pos[pend_ot[qi][0]] + 5):
                _, p_, jg_ = pend_ot[qi]
                out_recs.append(dict(kind="ot", p=p_, jg=jg_))
                qi += 1
        while qi < len(pend_ot):
            while len(out_recs) < src_pos[pend_ot[qi][0]] + 4:
                out_recs.append(dict(kind="pad"))
            _, p_, jg_ = pend_ot[qi]
            out_recs.append(dict(kind="ot", p=p_, jg=jg_))
            qi += 1
        recs = out_recs
        for k, r in enumerate(recs):
            r["cp"] = k % 2
            r["k"] = k
        st_p2a["fin"] = {}

        def beatA(r):
            if r["kind"] != "c":
                return
            S = ps_s[r["cp"]]
            p, h, n = r["p"], r["h"], r["n"]
            smm = []
            for sh in range(2):
                dep = [qt_last, last_kv[0]]
                if sh == 0:
                    dep += s_rel.get(r["cp"], [])
                smm.append(add("t", lambda e, p=p, h=h, n=n, sh=sh, S=S:
                               e.matmul(
                    S[:, sh * 512:(sh + 1) * 512],
                    BUF[:, h * OWN + p * 128: h * OWN + (p + 1) * 128],
                    KT[:, n * 1024 + sh * 512: n * 1024 + (sh + 1) * 512],
                    start=True, stop=True), dep, "smm"))
            r["ms"] = smm

        def beatB(r):
            if r["kind"] != "c":
                return
            S = ps_s[r["cp"]]
            ph, n, cp = r["ph"], r["n"], r["cp"]
            p = r["p"]
            ms_after = r["ms"]
            if n == r["nch"] - 1:
                mev = add("v", lambda e, p=p, S=S: e.tensor_tensor(
                    S[:, :], S[:, :],
                    mask_sb[:, p * 1024:(p + 1) * 1024], ALU.add),
                    ms_after + [ld_mask], "mask")
                ms_after = [mev]
            if n == 0:
                rmx = add("v", lambda e, S=S, ph=ph: e.reduce_max(
                    mneg[ph], S[:, :], axis=AX.X, negate=True),
                    ms_after + [eev_last.get(ph), mneg_last.get(ph)], "rmax")
                bias_ready = rmx
                r["aev"] = None
                add("s", lambda e, ph=ph: e.mul(
                    mpos[ph], mneg[ph], -1.0), [rmx], "mpos")
            else:
                rmx = add("v", lambda e, S=S, ph=ph: e.reduce_max(
                    mtmp[ph], S[:, :], axis=AX.X, negate=True),
                    ms_after + [mneg_last.get(ph)], "rmax")
                bias_ready = add("v", lambda e, ph=ph: e.tensor_tensor(
                    mneg[ph], mneg[ph], mtmp[ph], ALU.min),
                    [rmx, eev_last.get(ph), mneg_last.get(ph)], "mmin")
                asl = r["k"] % 4
                r["aev"] = add("s", lambda e, asl=asl, ph=ph: e.activation(
                    alpha[asl], mpos[ph], AF.Exp, bias=mneg[ph]),
                    [bias_ready, alpha_rel.get(asl)], "alpha")
                add("s", lambda e, ph=ph: e.mul(
                    mpos[ph], mneg[ph], -1.0), [r["aev"]], "mpos")
            mneg_last[ph] = bias_ready
            eev = add("s", lambda e, S=S, cp=cp, ph=ph: e.activation(
                p_sb[cp], S[:, :], AF.Exp, bias=mneg[ph]),
                ms_after + [bias_ready, p_rel.get(cp)], "exp")
            eev_last[ph] = eev
            s_rel[cp] = [eev, rmx]
            r["eev"] = eev

        def beatB2(r):
            if r["kind"] == "pad":
                return
            cp = r["cp"]
            if r["kind"] == "ot":
                p, jg = r["p"], r["jg"]
                fin = st_p2a["fin"][p]
                trs = []
                for s in range(8):
                    j = jg * 8 + s
                    trs.append(add("t", lambda e, j=j, s=s, cp=cp, pp=p % 2:
                                   e.transpose(
                        pb_t[cp][:, s * 128:(s + 1) * 128],
                        o_sbs[pp][:, j * 128:(j + 1) * 128], eye16),
                        [fin, ld_eye16, pb_rel.get(cp)], "ottr"))
                r["trs"] = trs
                return
            trs = []
            for s in range(8):
                trs.append(add("t", lambda e, s=s, cp=cp: e.transpose(
                    pb_t[cp][:, s * 128:(s + 1) * 128],
                    p_sb[cp][:, s * 128:(s + 1) * 128], eye16),
                    [r["eev"], ld_eye16, pb_rel.get(cp)], "ptr"))
            p_rel[cp] = trs[-1]
            r["trs"] = trs

        def beatC(r):
            if r["kind"] == "pad":
                return
            cp = r["cp"]
            if r["kind"] == "ot":
                p, jg = r["p"], r["jg"]
                da = BUF[:, :].rearrange("q (j c) -> q j c", c=OWN)[
                    :, jg * 8:(jg + 1) * 8, p * 128:(p + 1) * 128]
                ev = add("v", lambda e, cp=cp, da=da: e.tensor_copy(
                    da, pb_t[cp][:, :]), r["trs"], "otev")
                pb_rel[cp] = ev
                st_p2a[("osb", p % 2)] = ev
                return
            ptev = add("v", lambda e, cp=cp: e.tensor_copy(
                pt_sb[cp], pb_t[cp][:, :]),
                r["trs"] + [pts_rel.get(cp)], "ptev")
            pb_rel[cp] = ptev
            r["ptev"] = ptev

        def beatD(r):
            if r["kind"] != "c":
                return
            cp, ph, n, p, h = r["cp"], r["ph"], r["n"], r["p"], r["h"]
            avs = None
            for s in range(8):
                a = n * 8 + s
                dep = [r["ptev"], last_kv[1], ld_ones]
                if s == 0:
                    dep.append(avps_rel.get(cp))
                avs = add("t", lambda e, s=s, a=a, cp=cp,
                          st_=(s == 0), sp_=(s == 7): e.matmul(
                    o_ps[cp][:, 0:129],
                    pt_sb[cp][:, s * 128:(s + 1) * 128],
                    VSB[:, a * VST: a * VST + 129],
                    start=st_, stop=sp_), dep, "av")
            pts_rel[cp] = avs
            if n == 0:
                cmb = add("v", lambda e, cp=cp, ph=ph: e.tensor_copy(
                    o_acc[ph][:, 0:129], o_ps[cp][:, 0:129]),
                    [avs, ofin_rel.get(ph)], "ocmb0")
            else:
                asl = r["k"] % 4
                cmb = add("v", lambda e, cp=cp, ph=ph, asl=asl:
                          e.scalar_tensor_tensor(
                    o_acc[ph][:, 0:129], o_acc[ph][:, 0:129],
                    alpha[asl], o_ps[cp][:, 0:129],
                    ALU.mult, ALU.add),
                    [avs, r["aev"], avps_rel.get(("acc", ph))], "ocmb")
                alpha_rel[asl] = cmb
            avps_rel[cp] = cmb
            avps_rel[("acc", ph)] = cmb
            if n == r["nch"] - 1:
                rev = add("v", lambda e, ph=ph: e.reciprocal(
                    recip[ph], o_acc[ph][:, 128:129]), [cmb], "recip")
                fin = add("v", lambda e, h=h, ph=ph, pp=p % 2:
                          e.tensor_scalar_mul(
                    o_sbs[pp][:, h * 128:(h + 1) * 128], o_acc[ph][:, 0:128],
                    recip[ph]), [rev, cmb, st_p2a.get(("osb", p % 2))], "ofin")
                ofin_rel[ph] = fin
                if h == NH - 1:
                    st_p2a["fin"][p] = fin

        NR = len(recs)
        for it in range(NR + 4):
            if it < NR:
                beatA(recs[it])
            if 1 <= it < NR + 1:
                beatB(recs[it - 1])
            if 2 <= it < NR + 2:
                beatB2(recs[it - 2])
            if 3 <= it < NR + 3:
                beatC(recs[it - 3])
            if 4 <= it < NR + 4:
                beatD(recs[it - 4])

        P.barrier("endP2a")

        wo_rel = {}
        att_rel = {}
        ps2_rel = {}
        ps2 = [ps_a[:, 0:512], ps_b[:, 0:512]]
        for m in range(NJ):
            wpar = m % 2
            src = d_wo.rearrange("(j p) c -> p j c", p=128)[
                :, :, m * 128:(m + 1) * 128]
            ldw = dma("gp", wo_sb[wpar], src, deps=[wo_rel.get(wpar)], tag="wo", stream=f"wo{wpar}")
            for half in range(NHALF):
                pp = (m * NHALF + half) % 2
                mm = None
                for j in range(NJ):
                    dep = [ldw]
                    if j == 0:
                        dep.append(ps2_rel.get(pp))
                    mm = add("t", lambda e, j=j, m=m, half=half, wpar=wpar, pp=pp:
                             e.matmul(
                        ps2[pp][:, 0:HT],
                        wo_sb[wpar][:, j * 128:(j + 1) * 128],
                        BUF[:, j * OWN + half * HT: j * OWN + (half + 1) * HT],
                        start=(j == 0), stop=(j == NJ - 1)), dep, "womm")
                wo_rel[wpar] = mm
                aev = add("v", lambda e, pp=pp: e.tensor_copy(
                    att_scr[pp][:, 0:HT], ps2[pp][:, 0:HT]),
                    [mm, att_rel.get(pp)], "attev")
                ps2_rel[pp] = aev
                sd_ = dma("sy", d_asp[m, :, half * HT:(half + 1) * HT],
                          att_scr[pp][:, 0:HT], deps=[aev], tag="aspill",
                          stream=f"as{pp}")
                att_rel[pp] = sd_

        P.barrier("endP2b")

        pf2 = [pf_a, pf_b]
        pf2_rel = {}
        prev_yn = None
        prev_rd = None
        for p in range(NPOS):
            lda = dma("sy", attnb,
                      d_asp.rearrange("m q c -> q m c")[:, :, p * 128:(p + 1) * 128],
                      deps=[prev_rd], tag="attld", stream="attld")
            ldx = dma("sy", xqt, d_xq[p], deps=[prev_rd], tag="xqld", stream="xqld")
            ad = None
            for mc in range(NJ):
                fp = mc % 2
                tr = add("t", lambda e, mc=mc, fp=fp: e.transpose(
                    pf2[fp][:, 0:128], attnb[:, mc * 128:(mc + 1) * 128], eye32),
                    [lda, ld_eye32, pf2_rel.get(fp)], "attr")
                ad = add("v", lambda e, mc=mc, fp=fp: e.tensor_tensor(
                    y_sb[:, mc * 128:(mc + 1) * 128], pf2[fp][:, 0:128],
                    xqt[:, mc * 128:(mc + 1) * 128], ALU.add),
                    [tr, ldx, prev_yn], "yadd")
                pf2_rel[fp] = ad
            ysd = dma("sy", d_ysp[p], y_sb, deps=[ad], tag="yspill", stream="ysp")
            ap9 = ln_chain(p % 2, y_sb, yn_sb, [ad, prev_yn, ysd])
            prev_rd = ad

            def dst_fn(jg, p=p):
                da = BUF[:, :].rearrange("q (j c) -> q j c", c=OWN)[
                    :, jg * 8:(jg + 1) * 8, p * 128:(p + 1) * 128]
                return da, []
            evs = transpose_tile(yn_sb, ap9, dst_fn, tag=f"y{p}")
            prev_yn = evs[-1]

        P.barrier("endP2c")

        psg = [[ps_a[:, 0:512], ps_a[:, 512:1024]],
               [ps_b[:, 0:512], ps_b[:, 512:1024]]]
        win_rel = {}
        wT_rel = {}
        psf_rel = {}
        hi_rel = {}
        silu_rel = {}
        wdt_rel = {}
        st_f1 = {}
        pending = []
        ub = 0

        def flush(nmax=1):
            while len(pending) > nmax:
                pending.pop(0)()

        for i_ffn in range(NI):
            i = i_ffn
            ipar = i % 2
            gu_last = [[None] * NHALF, [None] * NHALF]
            for mi, dw in enumerate((d_wg, d_wu, d_wd)):
                for cg in range(4):
                    buf = ub % 4
                    ub += 1
                    if mi < 2:
                        wsrc = dw[i * 128:(i + 1) * 128,
                                  cg * 1024:(cg + 1) * 1024]
                    else:
                        wsrc = dw.rearrange("(j q) c -> q j c", q=128)[
                            :, cg * 8:(cg + 1) * 8, i * 128:(i + 1) * 128]
                    ldw = dma("gp", wbf_in[buf], wsrc,
                              deps=[win_rel.get(buf)], tag="wld",
                              stream=f"wl{buf}")
                    pbp = buf % 2
                    trs = []
                    for s in range(8):
                        trs.append(add("t", lambda e, s=s, buf=buf, pbp=pbp:
                                       e.transpose(
                            pb_t[pbp][:, s * 128:(s + 1) * 128],
                            wbf_in[buf][:, s * 128:(s + 1) * 128], eye16),
                            [ldw, ld_eye16, pb_rel.get(pbp)], "wtr"))
                    win_rel[buf] = trs[-1]
                    if mi < 2:
                        dst = wT[buf]
                        xd = [wT_rel.get(buf)]
                    else:
                        dst = wdtcg[cg % 2]
                        xd = [wdt_rel.get(cg % 2)]
                    tev = add("v", lambda e, dst=dst, pbp=pbp: e.tensor_copy(
                        dst, pb_t[pbp][:, :]), trs + xd, "wtev")
                    pb_rel[pbp] = tev

                    if mi == 2:
                        wd_dma = dma("sy", d_wdt[i, :, cg * 1024:(cg + 1) * 1024],
                                     wdtcg[cg % 2], deps=[tev],
                                     tag="wdtout", stream=f"wd{cg % 2}")
                        wdt_rel[cg % 2] = wd_dma
                        continue

                    def mk_mm(i=i, mi=mi, cg=cg, buf=buf, tev=tev,
                              gu_last=gu_last):
                        def emit():
                            mm = None
                            for half in range(NHALF):
                                hp = half % 2
                                for s in range(8):
                                    j = cg * 8 + s
                                    dep = [tev]
                                    if j == 0:
                                        dep.append(psf_rel.get((mi, hp)))
                                    mm = add("t", lambda e, s=s, j=j, mi=mi,
                                             buf=buf, half=half, hp=hp:
                                             e.matmul(
                                        psg[mi][hp][:, 0:HT],
                                        wT[buf][:, s * 128:(s + 1) * 128],
                                        BUF[:, j * OWN + half * HT:
                                            j * OWN + (half + 1) * HT],
                                        start=(j == 0), stop=(j == NJ - 1)),
                                        dep, "gumm")
                                gu_last[mi][half] = mm
                            wT_rel[buf] = mm
                        return emit
                    pending.append(mk_mm())
                    flush()

            def mk_epi(i=i, ipar=ipar, gu_last=gu_last):
                def emit():
                    nonlocal_state = None
                    hlast = None
                    for half in range(NHALF):
                        hp = half % 2
                        sev = add("s", lambda e, i=i, hp=hp: e.activation(
                            silu_scr[hp][:, 0:HT], psg[0][hp][:, 0:HT],
                            AF.Silu, scale=sg_sb[:, i:i + 1]),
                            [gu_last[0][half], ld_sg, silu_rel.get(hp)],
                            "silu")
                        hev = add("v", lambda e, i=i, hp=hp, ipar=ipar,
                                  half=half: e.scalar_tensor_tensor(
                            h_i[ipar][:, half * HT:(half + 1) * HT],
                            psg[1][hp][:, 0:HT], su_sb[:, i:i + 1],
                            silu_scr[hp][:, 0:HT], ALU.mult, ALU.mult),
                            [sev, gu_last[1][half], ld_su,
                             hi_rel.get(ipar) if half == 0 else None], "hmul")
                        silu_rel[hp] = hev
                        psf_rel[(0, hp)] = sev
                        psf_rel[(1, hp)] = hev
                        hlast = hev
                    cres = add("v", lambda e, i=i, ipar=ipar: e.tensor_copy(
                        h_all[:, i * GTP:(i + 1) * GTP],
                        h_i[ipar][:, 0:GTP]), [hlast, st_f1.get("hres")],
                        "hres")
                    st_f1["hres"] = cres
                    rel = cres
                    if NPASS > 1:
                        rel = dma("sy", d_hsp[i], h_i[ipar][:, GTP:OWN],
                                  deps=[hlast], tag="hspill",
                                  stream=f"hs{ipar}")
                    hi_rel[ipar] = rel
                return emit
            pending.append(mk_epi())
            flush()
        flush(0)

        P.barrier("endF1")

        psd = [ps_a[:, 0:GTP], ps_b[:, 0:GTP]]
        psd_rel = {}
        strip_rel = {}
        y_rel = {}
        orel = {}
        d_rel = {}
        pf3_rel = {}
        pass_last = None
        for ipass in range(NPASS):
            hload = None
            if ipass > 0:
                hsrc = d_hsp.rearrange("i q c -> q i c")[
                    :, :, (ipass - 1) * GTP: ipass * GTP]
                hload = dma("sy", h_all, hsrc, deps=[pass_last], tag="hl",
                            stream="hl")
            for j in range(NJ):
                sp = j % 2
                lds = dma("sy", strip[sp],
                          d_wdt.rearrange("i q c -> q i c")[
                              :, :, j * 128:(j + 1) * 128],
                          deps=[strip_rel.get(sp)], tag="strip",
                          stream=f"st{sp}")
                gp_ = j % 2
                mm = None
                for i in range(NI):
                    dep = [lds, hload]
                    if i == 0:
                        dep.append(psd_rel.get(gp_))
                    mm = add("t", lambda e, i=i, sp=sp, gp_=gp_: e.matmul(
                        psd[gp_], strip[sp][:, i * 128:(i + 1) * 128],
                        h_all[:, i * GTP:(i + 1) * GTP],
                        start=(i == 0), stop=(i == NI - 1)), dep, "dmm")
                strip_rel[sp] = mm
                pass_last = mm
                dev = add("v", lambda e, j=j, gp_=gp_: e.tensor_scalar_mul(
                    dscr[gp_], psd[gp_], sd_sb[:, j:j + 1]),
                    [mm, ld_sd, d_rel.get(gp_)], "dscl")
                psd_rel[gp_] = dev
                last_tr = None
                for tt in range(GTP // 128):
                    ptile = ipass * (GTP // 128) + tt
                    fp = (j * (GTP // 128) + tt) % 2
                    tr = add("t", lambda e, tt=tt, gp_=gp_, fp=fp:
                             e.transpose(
                        pf2[fp][:, 0:128],
                        dscr[gp_][:, tt * 128:(tt + 1) * 128], eye32),
                        [dev, ld_eye32, pf3_rel.get(fp)], "dtr")
                    last_tr = tr
                    yld = dma("gp", yscr[fp],
                              d_ysp[ptile, :, j * 128:(j + 1) * 128],
                              deps=[y_rel.get(fp)], tag="yld",
                              stream=f"yl{fp}")
                    oev = add("v", lambda e, fp=fp: e.tensor_tensor(
                        oscr[fp], pf2[fp][:, 0:128], yscr[fp], ALU.add),
                        [tr, yld, orel.get(fp)], "oadd")
                    pf3_rel[fp] = oev
                    y_rel[fp] = oev
                    od = dma("gp", d_out[ptile, :, j * 128:(j + 1) * 128],
                             oscr[fp], deps=[oev], tag="outw",
                             stream=f"ow{fp}")
                    orel[fp] = od
                d_rel[gp_] = last_tr

                P.barrier("final")

        run_engine = P.emit_fns(get_sem)
        with nc.Block() as blk:
            blk.sync(lambda e: run_engine(e, "sy"))
            blk.gpsimd(lambda e: run_engine(e, "gp"))
            blk.tensor(lambda e: run_engine(e, "t"))
            blk.vector(lambda e: run_engine(e, "v"))
            blk.scalar(lambda e: run_engine(e, "s"))

    return nc


def _own_tiles(c, NT):
    na = NT // NCORES // 2
    return ([c + 8 * j for j in range(na)]
            + [NT - 1 - c - 8 * j for j in range(na)])


def _masks_for_core(c, NT):
    NPOS = NT // NCORES
    na = NPOS // 2
    m = np.zeros((NPOS, 128, 1024), dtype=np.float32)
    tiles = _own_tiles(c, NT)
    tri = np.where(np.arange(1024)[None, :] % 128 <= np.arange(128)[:, None],
                   0.0, MASK_NEG).astype(np.float32)
    for p, t in enumerate(tiles):
        bound = 8 * (p + 1) if p < na else NT - 8 * (p - na)
        for s in range(8):
            abs_sub = bound - 8 + s
            if abs_sub < t:
                continue
            if abs_sub == t:
                m[p, :, s * 128:(s + 1) * 128] = tri[:, s * 128:(s + 1) * 128]
            else:
                m[p, :, s * 128:(s + 1) * 128] = MASK_NEG
    return m


def make_inputs(x, wq, wk, wv, wo, wg, sgv, wu, suv, wd, sdv, T, I_FFN):
    NT = T // 128
    NI = I_FFN // 128
    base = {
        "x": np.ascontiguousarray(x.reshape(T, D)),
        "wq": np.ascontiguousarray(wq), "wk": np.ascontiguousarray(wk),
        "wv": np.ascontiguousarray(wv), "wo": np.ascontiguousarray(wo),
        "wg": np.ascontiguousarray(wg), "wu": np.ascontiguousarray(wu),
        "wd": np.ascontiguousarray(wd),
        "sg": np.ascontiguousarray(sgv.reshape(NI, 128).T),
        "su": np.ascontiguousarray(suv.reshape(NI, 128).T),
        "sd": np.ascontiguousarray(sdv.reshape(NJ, 128).T),
        "eye16": np.eye(128, dtype=ml_dtypes.bfloat16),
        "eye32": np.eye(128, dtype=np.float32),
    }
    xr = base["x"].reshape(NT, 128, D)
    ins = []
    for c in range(NCORES):
        m = dict(base)
        m["xq"] = np.ascontiguousarray(xr[_own_tiles(c, NT)])
        m["mask"] = _masks_for_core(c, NT)
        ins.append(m)
    return ins


def assemble_output(results, T):
    NT = T // 128
    out = np.zeros((NT, 128, D), dtype=np.float32)
    for c in range(NCORES):
        out[_own_tiles(c, NT)] = results[c]["out"]
    return out.reshape(1, T, D)


def kernel(L0_attn_x, L0_attn_wq, L0_attn_wk, L0_attn_wv, L0_attn_wo,
           L0_ffn_w_gate, L0_ffn_s_gate, L0_ffn_w_up, L0_ffn_s_up,
           L0_ffn_w_down, L0_ffn_s_down):
    T = int(L0_attn_x.shape[1])
    I_FFN = int(L0_ffn_w_gate.shape[0])
    nc = build_nc(T=T, I_FFN=I_FFN)
    ins = make_inputs(np.asarray(L0_attn_x, dtype=np.float32).reshape(T, D),
                      np.asarray(L0_attn_wq, dtype=np.float32),
                      np.asarray(L0_attn_wk, dtype=np.float32),
                      np.asarray(L0_attn_wv, dtype=np.float32),
                      np.asarray(L0_attn_wo, dtype=np.float32),
                      np.asarray(L0_ffn_w_gate, dtype=np.int32),
                      np.asarray(L0_ffn_s_gate, dtype=np.float32),
                      np.asarray(L0_ffn_w_up, dtype=np.int32),
                      np.asarray(L0_ffn_s_up, dtype=np.float32),
                      np.asarray(L0_ffn_w_down, dtype=np.int32),
                      np.asarray(L0_ffn_s_down, dtype=np.float32),
                      T, I_FFN)
    res = run_bass_kernel_spmd(nc, ins, core_ids=list(range(NCORES)))
    return assemble_output(res.results, T)


# version 08
# speedup vs baseline: 1.1048x; 1.1048x over previous
import numpy as np
import ml_dtypes
from contextlib import ExitStack

import concourse.bass as bass
import concourse.mybir as mybir
from concourse.bass_utils import run_bass_kernel_spmd

F32 = mybir.dt.float32
BF16 = mybir.dt.bfloat16
I32 = mybir.dt.int32
U8 = mybir.dt.uint8
AF = mybir.ActivationFunctionType
ALU = mybir.AluOpType
AX = mybir.AxisListType

NCORES = 8
D = 4096
NH = 32
DH = 128
NJ = D // 128
EPS = 1e-5
QSCALE = 1.0 / float(np.sqrt(DH))
MASK_NEG = -30000.0

ENGS = ("sy", "gp", "t", "v", "s")
CHAIN_ENGS = ()
DECLARED_SELF = ("v", "s")


class Prog:

    def __init__(self):
        self.ev = []
        self.last_of = {e: None for e in ENGS}
        self.last_stream = {}

    def add(self, eng, fn, deps=(), tag="", stream=None):
        i = len(self.ev)
        deps = [d for d in deps if d is not None]
        if stream is not None:
            prev = self.last_stream.get((eng, stream))
            if prev is not None:
                deps.append(prev)
            self.last_stream[(eng, stream)] = i
        self.ev.append(dict(i=i, eng=eng, fn=fn, deps=deps, tag=tag,
                            stream=stream, need=False, val=None, sem=None))
        self.last_of[eng] = i
        return i

    def barrier(self, tag="bar"):
        lasts = [self.last_of[e] for e in ENGS if self.last_of[e] is not None]
        lasts += [v for v in self.last_stream.values()]
        out = {}
        for e in ENGS:
            out[e] = self.add(e, None, deps=list(lasts), tag=tag)
        return out

    def emit_fns(self, get_sem):
        ev = self.ev
        for e_ in ev:
            for d in e_["deps"]:
                ev[d]["need"] = True
        counters = {}
        for e_ in ev:
            eng = e_["eng"]
            if eng in ("sy", "gp"):
                if e_["fn"] is not None:
                    sname = f"D:{eng}:{e_['stream'] or 'def'}"
                    counters[sname] = counters.get(sname, 0) + 16
                    e_["sem"], e_["val"] = sname, counters[sname]
                else:
                    sname = eng + "m"
                    counters[sname] = counters.get(sname, 0) + 1
                    e_["sem"], e_["val"] = sname, counters[sname]
            elif eng in CHAIN_ENGS or e_["need"] or e_["fn"] is None:
                counters[eng] = counters.get(eng, 0) + 1
                e_["sem"], e_["val"] = eng, counters[eng]
        per_eng = {e: [] for e in ENGS}
        for e_ in ev:
            per_eng[e_["eng"]].append(e_)

        def run_engine(handle, eng):
            satisfied = {}
            prev_val = None
            for e_ in per_eng[eng]:
                waits = {}
                for d in e_["deps"]:
                    p = ev[d]
                    if p["val"] is None:
                        continue
                    if p["eng"] == eng and not p["sem"].startswith("D:"):
                        if eng not in DECLARED_SELF:
                            continue
                    if (p["eng"] == eng and e_["fn"] is not None
                            and e_["sem"] == p["sem"]):
                        pass
                    sname = p["sem"]
                    if satisfied.get(sname, 0) >= p["val"]:
                        continue
                    waits[sname] = max(waits.get(sname, 0), p["val"])
                if eng in CHAIN_ENGS and prev_val is not None:
                    waits[eng] = max(waits.get(eng, 0), prev_val)
                for se, v in sorted(waits.items()):
                    handle.wait_ge(get_sem(se), v)
                    satisfied[se] = v
                if e_["fn"] is not None:
                    inst = e_["fn"](handle)
                    if e_["val"] is not None:
                        inst.then_inc(get_sem(e_["sem"]),
                                      16 if e_["sem"].startswith("D:") else 1)
                elif e_["val"] is not None:
                    handle.nop().then_inc(get_sem(e_["sem"]), 1)
                if e_["val"] is not None and e_["sem"] == eng:
                    prev_val = e_["val"]
        return run_engine


def build_nc(T=8192, I_FFN=11008):
    NT = T // 128
    NPOS = NT // NCORES
    NA = NPOS // 2
    OWN = NPOS * 128
    NI = I_FFN // 128
    GTP = min(512, OWN)
    NPASS = OWN // GTP
    NHALF = max(1, OWN // 512)
    HT = OWN // NHALF

    def nch(p):
        return (p + 1) if p < NA else (NT // 8 - (p - NA))

    nc = bass.Bass("TRN2", target_bir_lowering=False)

    d_x = nc.dram_tensor("x", [T, D], F32, kind="ExternalInput").ap()
    d_xq = nc.dram_tensor("xq", [NPOS, 128, D], F32, kind="ExternalInput").ap()
    d_wq = nc.dram_tensor("wq", [D, D], F32, kind="ExternalInput").ap()
    d_wk = nc.dram_tensor("wk", [D, DH], F32, kind="ExternalInput").ap()
    d_wv = nc.dram_tensor("wv", [D, DH], F32, kind="ExternalInput").ap()
    d_wo = nc.dram_tensor("wo", [D, D], F32, kind="ExternalInput").ap()
    d_wg = nc.dram_tensor("wg", [I_FFN, D], I32, kind="ExternalInput").ap()
    d_wu = nc.dram_tensor("wu", [I_FFN, D], I32, kind="ExternalInput").ap()
    d_wd = nc.dram_tensor("wd", [D, I_FFN], I32, kind="ExternalInput").ap()
    d_sg = nc.dram_tensor("sg", [128, NI], F32, kind="ExternalInput").ap()
    d_su = nc.dram_tensor("su", [128, NI], F32, kind="ExternalInput").ap()
    d_sd = nc.dram_tensor("sd", [128, NJ], F32, kind="ExternalInput").ap()
    d_eye16 = nc.dram_tensor("eye16", [128, 128], BF16, kind="ExternalInput").ap()
    d_eye32 = nc.dram_tensor("eye32", [128, 128], F32, kind="ExternalInput").ap()
    d_mask = nc.dram_tensor("mask", [NPOS, 128, 1024], F32, kind="ExternalInput").ap()
    d_out = nc.dram_tensor("out", [NPOS, 128, D], F32, kind="ExternalOutput").ap()
    d_wdt = nc.dram_tensor("wdt_s", [NI, 128, D], BF16).ap()
    d_hsp = (nc.dram_tensor("h_sp", [NI, 128, OWN - GTP], BF16).ap()
             if NPASS > 1 else None)
    d_ysp = nc.dram_tensor("y_sp", [NPOS, 128, D], F32).ap()
    d_asp = nc.dram_tensor("attn_sp", [NJ, 128, OWN], F32).ap()

    P = Prog()
    ARENA = 204800

    with ExitStack() as stk:
        en = stk.enter_context
        AR = en(nc.sbuf_tensor("AR", [128, ARENA], U8))
        arf = AR[:, :].bitcast(F32)
        arb = AR[:, :].bitcast(BF16)

        def fbuf(off, cols):
            assert off % 4 == 0 and off + 4 * cols <= ARENA, (off, cols)
            return arf[:, off // 4: off // 4 + cols]

        def bbuf(off, cols):
            assert off % 2 == 0 and off + 2 * cols <= ARENA, (off, cols)
            return arb[:, off // 2: off // 2 + cols]

        eye16 = bbuf(0, 128)
        eye32 = fbuf(512, 128)
        sg_sb = fbuf(1024, NI)
        su_sb = fbuf(1536, NI)
        sd_sb = fbuf(2048, NJ)
        ST = 2560

        def stat(slot):
            return fbuf(ST + 4 * slot, 1)
        wk_sb = bbuf(3072, NJ * 128)
        wv_sb = bbuf(11264, NJ * 128)
        BUFO = 19456
        BUF = bbuf(BUFO, NH * OWN)
        PH = BUFO + NH * OWN * 2
        KT = bbuf(PH, NT * 128)
        VST = 132
        VSB = bbuf(PH + NT * 256, NT * VST)
        PA = PH + NT * 256 + ((NT * VST * 2 + 511) // 512) * 512
        xt = [fbuf(PA + i * 16384, D) for i in range(2)]
        xn = [bbuf(PA + 32768 + i * 8192, D) for i in range(2)]
        xnT = [bbuf(PA + 49152 + i * 8192, D) for i in range(2)]
        wq_sb = [bbuf(PA + 16384 + i * 8192, 8 * 512) for i in range(2)]
        xnqT = bbuf(PA + 49152, NJ * HT)
        mask_sb = fbuf(PA, NPOS * 1024)
        p_sb = [bbuf(PA + 32768 + i * 2048, 1024) for i in range(2)]
        pt_sb = [bbuf(PA + 36864 + i * 2048, 1024) for i in range(2)]
        o_sbs = [bbuf(PA + 40960, D), bbuf(PA + 51712, D)]
        o_acc = [fbuf(PA + 49152 + i * 640, VST) for i in range(4)]
        wo_sb = [bbuf(PH + i * 8192, NJ * 128) for i in range(2)]
        att_scr = [fbuf(PH + 16384 + i * 2048, 512) for i in range(2)]
        attnb = fbuf(PA, D)
        xqt = fbuf(PA + 16384, D)
        y_sb = fbuf(PA + 32768, D)
        yn_sb = bbuf(PA + 49152, D)
        h_all = bbuf(PH, NI * GTP)
        F1B = PH + ((NI * GTP * 2 + 511) // 512) * 512
        wbf_in = [bbuf(F1B + b * 2048, 1024) for b in range(4)]
        wT = [bbuf(F1B + 8192 + b * 2048, 1024) for b in range(4)]
        wdtcg = [bbuf(F1B + 16384 + b * 2048, 1024) for b in range(2)]
        h_i = [bbuf(F1B + 20480 + b * ((OWN * 2 + 511) // 512) * 512, OWN)
               for b in range(2)]
        HIE = F1B + 20480 + 2 * ((OWN * 2 + 511) // 512) * 512
        silu_scr = [fbuf(HIE + b * 2048, 512) for b in range(2)]
        assert HIE + 4096 <= ARENA, (HIE, ARENA)
        strip = [bbuf(BUFO + i * NI * 256, NI * 128) for i in range(2)]
        SS = BUFO + 2 * NI * 256
        dscr = [fbuf(SS + i * 2048, GTP) for i in range(2)]
        yscr = [fbuf(SS + 4096 + i * 512, 128) for i in range(2)]
        oscr = [fbuf(SS + 5120 + i * 512, 128) for i in range(2)]
        assert SS + 6144 <= BUFO + NH * OWN * 2

        ps_a = en(nc.psum_tensor("ps_a", [128, 1024], F32))
        ps_b = en(nc.psum_tensor("ps_b", [128, 1024], F32))
        pf_a = en(nc.psum_tensor("pf_a", [128, 512], F32))
        pf_b = en(nc.psum_tensor("pf_b", [128, 512], F32))
        pb_t = [en(nc.psum_tensor(f"pb{i}", [128, 1024], BF16)) for i in range(2)]
        sem_cache = {}

        def get_sem(name):
            if name not in sem_cache:
                sem_cache[name] = en(nc.semaphore(
                    "sem_" + name.replace(":", "_")))
            return sem_cache[name]

        def dma(q, out_ap, in_ap, deps=(), tag="", stream=None):
            return P.add(q, lambda e, o=out_ap, i=in_ap: e.dma_start(o, i),
                         deps, tag, stream=(stream or tag or "def"))

        add = P.add

        ld_eye16 = dma("sy", eye16, d_eye16, tag="eye16", stream="c")
        ld_eye32 = dma("sy", eye32, d_eye32, tag="eye32", stream="c")
        ld_sg = dma("sy", sg_sb, d_sg, stream="c")
        ld_su = dma("sy", su_sb, d_su, stream="c")
        ld_sd = dma("sy", sd_sb, d_sd, stream="c")
        ld_wk = dma("gp", wk_sb, d_wk.rearrange("(j p) c -> p j c", p=128), stream="gc")
        ld_wv = dma("gp", wv_sb, d_wv.rearrange("(j p) c -> p j c", p=128), stream="gc")
        ld_ones = None
        eps_ap = stat(60)
        ld_eps = add("v", lambda e: e.memset(eps_ap, float(EPS)), [], "eps")

        def ln_chain(par, src, dst_bf, deps_in):
            s_sum, s_sq = stat(0 + par), stat(2 + par)
            s_mean, s_var = stat(4 + par), stat(6 + par)
            s_sd, s_rstd = stat(8 + par), stat(10 + par)
            s_negmr, s_m2 = stat(12 + par), stat(14 + par)
            e1 = add("v", lambda e: e.reduce_sum(s_sum, src, axis=AX.X),
                     deps_in, "ln_sum")
            e2 = add("s", lambda e: e.activation(dst_bf, src, AF.Square,
                                                 accum_out=s_sq),
                     deps_in, "ln_sq")
            e3 = add("v", lambda e: e.tensor_scalar_mul(s_mean, s_sum, 1.0 / D),
                     [e1], "ln_mean")
            e4 = add("v", lambda e: e.tensor_mul(s_m2, s_mean, s_mean), [e3])
            e5 = add("v", lambda e: e.scalar_tensor_tensor(
                s_var, s_sq, 1.0 / D, s_m2, ALU.mult, ALU.subtract), [e2, e4])
            e6 = add("s", lambda e: e.activation(s_sd, s_var, AF.Sqrt,
                                                 bias=eps_ap), [e5, ld_eps])
            e7 = add("v", lambda e: e.reciprocal(s_rstd, s_sd), [e6])
            e8 = add("v", lambda e: e.scalar_tensor_tensor(
                s_negmr, s_mean, -1.0, s_rstd, ALU.mult, ALU.mult), [e7, e3])
            e9 = add("s", lambda e: e.activation(dst_bf, src, AF.Identity,
                                                 bias=s_negmr, scale=s_rstd),
                     [e8], "ln_apply")
            return e9

        pb_rel = {}

        def transpose_tile(src_bf, dep_src, dst_fn, tag=""):
            evs = []
            for jg in range(NJ // 8):
                pbk = pb_t[jg % 2]
                rel = pb_rel.get(jg % 2)
                trs = []
                for s in range(8):
                    j = jg * 8 + s
                    trs.append(add("t", lambda e, j=j, s=s, p=pbk:
                                   e.transpose(p[:, s * 128:(s + 1) * 128],
                                               src_bf[:, j * 128:(j + 1) * 128],
                                               eye16),
                                   [dep_src, ld_eye16, rel], tag + "tr"))
                dst_ap, xdeps = dst_fn(jg)
                ev = add("v", lambda e, p=pbk, da=dst_ap: e.tensor_copy(da, p[:, :]),
                         trs + list(xdeps), tag + "ev")
                pb_rel[jg % 2] = ev
                evs.append(ev)
            return evs

        ld_ones = add("v", lambda e: e.memset(
            VSB[:, :].rearrange("q (t c) -> q t c", c=VST)[:, :, 128:129], 1.0),
            [], "vones")
        xt_rel = {}
        xn_rel = {}
        xnT_rel = {}
        kv_rel = {}
        last_kv = [None, None]
        for i in range(NT):
            par = i % 2
            ld = dma("sy", xt[par], d_x[i * 128:(i + 1) * 128, :],
                     deps=[xt_rel.get(par)], tag=f"x{i}", stream=f"x{par}")
            ap9 = ln_chain(par, xt[par], xn[par], [ld, xn_rel.get(par)])
            xt_rel[par] = ap9

            def dst_fn(jg, par=par):
                return xnT[par][:, jg * 1024:(jg + 1) * 1024], \
                    ([xnT_rel.get(par)] if xnT_rel.get(par) is not None else [])
            evacs = transpose_tile(xn[par], ap9, dst_fn, tag=f"a{i}")
            xn_rel[par] = evacs[-1]
            kmm = vmm = None
            for j in range(NJ):
                dep = [evacs[j // 8], ld_wk]
                if j == 0:
                    dep.append(kv_rel.get("k"))
                kmm = add("t", lambda e, j=j, par=par: e.matmul(
                    pf_a[:, 0:128], wk_sb[:, j * 128:(j + 1) * 128],
                    xnT[par][:, j * 128:(j + 1) * 128],
                    start=(j == 0), stop=(j == NJ - 1)), dep, "kmm")
            kev = add("v", lambda e, i=i: e.tensor_copy(
                KT[:, i * 128:(i + 1) * 128], pf_a[:, 0:128]), [kmm], "kev")
            kv_rel["k"] = kev
            for j in range(NJ):
                dep = [evacs[j // 8], ld_wv]
                if j == 0:
                    dep.append(kv_rel.get("v"))
                vmm = add("t", lambda e, j=j, par=par: e.matmul(
                    pf_b[:, 0:128], xnT[par][:, j * 128:(j + 1) * 128],
                    wv_sb[:, j * 128:(j + 1) * 128],
                    start=(j == 0), stop=(j == NJ - 1)), dep, "vmm")
            vev = add("v", lambda e, i=i: e.tensor_copy(
                VSB[:, i * VST:i * VST + 128], pf_b[:, 0:128]), [vmm], "vev")
            kv_rel["v"] = vev
            xnT_rel[par] = vmm
            last_kv = [kev, vev]

        P.barrier("endP1a")

        qt_last = None
        psq = [ps_a[:, 0:HT], ps_a[:, 512:512 + HT],
               ps_b[:, 0:HT], ps_b[:, 512:512 + HT]]
        psq_rel = [None] * 4
        wq_rel = {}
        xq_rel = None
        for half in range(NHALF):
            ev_groups = {jg: [] for jg in range(4)}
            prev_ln = None
            for ti in range(HT // 128):
                p = half * (HT // 128) + ti
                ld = dma("sy", xt[0], d_xq[p], deps=[xq_rel], tag=f"xq{p}", stream="xq")
                ap9 = ln_chain(p % 2, xt[0], xn[p % 2], [ld, prev_ln])
                xq_rel = ap9

                def dst_fn(jg, ti=ti, half=half):
                    da = xnqT[:, :].rearrange("q (j c) -> q j c", c=HT)[
                        :, jg * 8:(jg + 1) * 8, ti * 128:(ti + 1) * 128]
                    xd = [qt_last] if (half > 0 and ti == 0) else []
                    return da, xd
                evacs = transpose_tile(xn[p % 2], ap9, dst_fn, tag=f"b{p}")
                prev_ln = evacs[-1]
                for jg in range(4):
                    ev_groups[jg].append(evacs[jg])
            for mb in range(8):
                last_mm = None
                for jg in range(4):
                    wpar = (mb * 4 + jg) % 2
                    src = d_wq.rearrange("(j p) c -> p j c", p=128)[
                        :, jg * 8:(jg + 1) * 8, mb * 512:(mb + 1) * 512]
                    ldw = dma("gp", wq_sb[wpar], src,
                              deps=[wq_rel.get(wpar)], tag="wq",
                              stream=f"wq{wpar}")
                    for jj in range(8):
                        j = jg * 8 + jj
                        for m in range(4):
                            dep = [ldw] + ev_groups[jg]
                            if j == 0:
                                dep.append(psq_rel[m])
                            st_ = (j == 0)
                            sp_ = (j == NJ - 1)
                            last_mm = add("t", lambda e, jj=jj, m=m, wpar=wpar,
                                          j=j, st_=st_, sp_=sp_: e.matmul(
                                psq[m],
                                wq_sb[wpar][:, jj * 512 + m * 128:
                                            jj * 512 + (m + 1) * 128],
                                xnqT[:, j * HT: (j + 1) * HT],
                                start=st_, stop=sp_), dep, "qmm")
                    wq_rel[wpar] = last_mm
                for m in range(4):
                    h = mb * 4 + m
                    ev = add("v", lambda e, h=h, m=m, half=half: e.tensor_scalar_mul(
                        BUF[:, h * OWN + half * HT: h * OWN + (half + 1) * HT],
                        psq[m], QSCALE), [last_mm], "qev")
                    psq_rel[m] = ev
                    qt_last = ev

        P.barrier("endP2b_pre")

        ld_mask = dma("sy", mask_sb, d_mask.rearrange("p q c -> q p c"),
                      tag="mask", stream="mask")
        ps_s = [ps_a, ps_b]
        o_ps = [pf_a, pf_b]
        NPH = 4
        mneg = [stat(16 + i) for i in range(NPH)]
        mpos = [stat(20 + i) for i in range(NPH)]
        alpha = [stat(24 + i) for i in range(4)]
        recip = [stat(36 + i) for i in range(NPH)]
        mtmp = [stat(40 + i) for i in range(NPH)]
        s_rel = {}
        p_rel = {}
        pts_rel = {}
        avps_rel = {}
        ofin_rel = {}
        eev_last = {}
        mneg_last = {}
        alpha_rel = {}
        st_p2a = {}

        recs = []
        hidx = 0
        pend_ot = []
        for p in range(NPOS):
            NCH = nch(p)
            for h in range(NH):
                for n in range(NCH):
                    recs.append(dict(kind="c", p=p, h=h, n=n, nch=NCH,
                                     ph=hidx % NPH))
                hidx += 1
            for jg in range(4):
                pend_ot.append((len(recs) - 1, p, jg))
        out_recs = []
        qi = 0
        src_pos = {}
        for i, r in enumerate(recs):
            out_recs.append(r)
            src_pos[i] = len(out_recs) - 1
            while (qi < len(pend_ot) and pend_ot[qi][0] <= i
                   and len(out_recs) >= src_pos[pend_ot[qi][0]] + 5):
                _, p_, jg_ = pend_ot[qi]
                out_recs.append(dict(kind="ot", p=p_, jg=jg_))
                qi += 1
        while qi < len(pend_ot):
            while len(out_recs) < src_pos[pend_ot[qi][0]] + 4:
                out_recs.append(dict(kind="pad"))
            _, p_, jg_ = pend_ot[qi]
            out_recs.append(dict(kind="ot", p=p_, jg=jg_))
            qi += 1
        recs = out_recs
        for k, r in enumerate(recs):
            r["cp"] = k % 2
            r["k"] = k
        st_p2a["fin"] = {}

        def beatA(r):
            if r["kind"] != "c":
                return
            S = ps_s[r["cp"]]
            p, h, n = r["p"], r["h"], r["n"]
            smm = []
            for sh in range(2):
                dep = [qt_last, last_kv[0]]
                if sh == 0:
                    dep += s_rel.get(r["cp"], [])
                smm.append(add("t", lambda e, p=p, h=h, n=n, sh=sh, S=S:
                               e.matmul(
                    S[:, sh * 512:(sh + 1) * 512],
                    BUF[:, h * OWN + p * 128: h * OWN + (p + 1) * 128],
                    KT[:, n * 1024 + sh * 512: n * 1024 + (sh + 1) * 512],
                    start=True, stop=True), dep, "smm"))
            r["ms"] = smm

        def beatB(r):
            if r["kind"] != "c":
                return
            S = ps_s[r["cp"]]
            ph, n, cp = r["ph"], r["n"], r["cp"]
            p = r["p"]
            ms_after = r["ms"]
            if n == r["nch"] - 1:
                mev = add("v", lambda e, p=p, S=S: e.tensor_tensor(
                    S[:, :], S[:, :],
                    mask_sb[:, p * 1024:(p + 1) * 1024], ALU.add),
                    ms_after + [ld_mask], "mask")
                ms_after = [mev]
            if n == 0:
                rmx = add("v", lambda e, S=S, ph=ph: e.reduce_max(
                    mneg[ph], S[:, :], axis=AX.X, negate=True),
                    ms_after + [eev_last.get(ph), mneg_last.get(ph)], "rmax")
                bias_ready = rmx
                r["aev"] = None
                add("s", lambda e, ph=ph: e.mul(
                    mpos[ph], mneg[ph], -1.0), [rmx], "mpos")
            else:
                rmx = add("v", lambda e, S=S, ph=ph: e.reduce_max(
                    mtmp[ph], S[:, :], axis=AX.X, negate=True),
                    ms_after + [mneg_last.get(ph)], "rmax")
                bias_ready = add("v", lambda e, ph=ph: e.tensor_tensor(
                    mneg[ph], mneg[ph], mtmp[ph], ALU.min),
                    [rmx, eev_last.get(ph), mneg_last.get(ph)], "mmin")
                asl = r["k"] % 4
                r["aev"] = add("s", lambda e, asl=asl, ph=ph: e.activation(
                    alpha[asl], mpos[ph], AF.Exp, bias=mneg[ph]),
                    [bias_ready, alpha_rel.get(asl)], "alpha")
                add("s", lambda e, ph=ph: e.mul(
                    mpos[ph], mneg[ph], -1.0), [r["aev"]], "mpos")
            mneg_last[ph] = bias_ready
            eev = add("s", lambda e, S=S, cp=cp, ph=ph: e.activation(
                p_sb[cp], S[:, :], AF.Exp, bias=mneg[ph]),
                ms_after + [bias_ready, p_rel.get(cp)], "exp")
            eev_last[ph] = eev
            s_rel[cp] = [eev, rmx]
            r["eev"] = eev

        def beatB2(r):
            if r["kind"] == "pad":
                return
            cp = r["cp"]
            if r["kind"] == "ot":
                p, jg = r["p"], r["jg"]
                fin = st_p2a["fin"][p]
                trs = []
                for s in range(8):
                    j = jg * 8 + s
                    trs.append(add("t", lambda e, j=j, s=s, cp=cp, pp=p % 2:
                                   e.transpose(
                        pb_t[cp][:, s * 128:(s + 1) * 128],
                        o_sbs[pp][:, j * 128:(j + 1) * 128], eye16),
                        [fin, ld_eye16, pb_rel.get(cp)], "ottr"))
                r["trs"] = trs
                return
            trs = []
            for s in range(8):
                trs.append(add("t", lambda e, s=s, cp=cp: e.transpose(
                    pb_t[cp][:, s * 128:(s + 1) * 128],
                    p_sb[cp][:, s * 128:(s + 1) * 128], eye16),
                    [r["eev"], ld_eye16, pb_rel.get(cp)], "ptr"))
            p_rel[cp] = trs[-1]
            r["trs"] = trs

        def beatC(r):
            if r["kind"] == "pad":
                return
            cp = r["cp"]
            if r["kind"] == "ot":
                p, jg = r["p"], r["jg"]
                da = BUF[:, :].rearrange("q (j c) -> q j c", c=OWN)[
                    :, jg * 8:(jg + 1) * 8, p * 128:(p + 1) * 128]
                ev = add("v", lambda e, cp=cp, da=da: e.tensor_copy(
                    da, pb_t[cp][:, :]), r["trs"], "otev")
                pb_rel[cp] = ev
                st_p2a[("osb", p % 2)] = ev
                return
            ptev = add("v", lambda e, cp=cp: e.tensor_copy(
                pt_sb[cp], pb_t[cp][:, :]),
                r["trs"] + [pts_rel.get(cp)], "ptev")
            pb_rel[cp] = ptev
            r["ptev"] = ptev

        def beatD(r):
            if r["kind"] != "c":
                return
            cp, ph, n, p, h = r["cp"], r["ph"], r["n"], r["p"], r["h"]
            avs = None
            for s in range(8):
                a = n * 8 + s
                dep = [r["ptev"], last_kv[1], ld_ones]
                if s == 0:
                    dep.append(avps_rel.get(cp))
                avs = add("t", lambda e, s=s, a=a, cp=cp,
                          st_=(s == 0), sp_=(s == 7): e.matmul(
                    o_ps[cp][:, 0:129],
                    pt_sb[cp][:, s * 128:(s + 1) * 128],
                    VSB[:, a * VST: a * VST + 129],
                    start=st_, stop=sp_), dep, "av")
            pts_rel[cp] = avs
            if n == 0:
                cmb = add("v", lambda e, cp=cp, ph=ph: e.tensor_copy(
                    o_acc[ph][:, 0:129], o_ps[cp][:, 0:129]),
                    [avs, ofin_rel.get(ph)], "ocmb0")
            else:
                asl = r["k"] % 4
                cmb = add("v", lambda e, cp=cp, ph=ph, asl=asl:
                          e.scalar_tensor_tensor(
                    o_acc[ph][:, 0:129], o_acc[ph][:, 0:129],
                    alpha[asl], o_ps[cp][:, 0:129],
                    ALU.mult, ALU.add),
                    [avs, r["aev"], avps_rel.get(("acc", ph))], "ocmb")
                alpha_rel[asl] = cmb
            avps_rel[cp] = cmb
            avps_rel[("acc", ph)] = cmb
            if n == r["nch"] - 1:
                rev = add("v", lambda e, ph=ph: e.reciprocal(
                    recip[ph], o_acc[ph][:, 128:129]), [cmb], "recip")
                fin = add("v", lambda e, h=h, ph=ph, pp=p % 2:
                          e.tensor_scalar_mul(
                    o_sbs[pp][:, h * 128:(h + 1) * 128], o_acc[ph][:, 0:128],
                    recip[ph]), [rev, cmb, st_p2a.get(("osb", p % 2))], "ofin")
                ofin_rel[ph] = fin
                if h == NH - 1:
                    st_p2a["fin"][p] = fin

        NR = len(recs)
        for it in range(NR + 4):
            if it < NR:
                beatA(recs[it])
            if 1 <= it < NR + 1:
                beatB(recs[it - 1])
            if 2 <= it < NR + 2:
                beatB2(recs[it - 2])
            if 3 <= it < NR + 3:
                beatC(recs[it - 3])
            if 4 <= it < NR + 4:
                beatD(recs[it - 4])

        P.barrier("endP2a")

        wo_rel = {}
        att_rel = {}
        ps2_rel = {}
        ps2 = [ps_a[:, 0:512], ps_b[:, 0:512]]
        for m in range(NJ):
            wpar = m % 2
            src = d_wo.rearrange("(j p) c -> p j c", p=128)[
                :, :, m * 128:(m + 1) * 128]
            ldw = dma("gp", wo_sb[wpar], src, deps=[wo_rel.get(wpar)], tag="wo", stream=f"wo{wpar}")
            for half in range(NHALF):
                pp = (m * NHALF + half) % 2
                mm = None
                for j in range(NJ):
                    dep = [ldw]
                    if j == 0:
                        dep.append(ps2_rel.get(pp))
                    mm = add("t", lambda e, j=j, m=m, half=half, wpar=wpar, pp=pp:
                             e.matmul(
                        ps2[pp][:, 0:HT],
                        wo_sb[wpar][:, j * 128:(j + 1) * 128],
                        BUF[:, j * OWN + half * HT: j * OWN + (half + 1) * HT],
                        start=(j == 0), stop=(j == NJ - 1)), dep, "womm")
                wo_rel[wpar] = mm
                aev = add("v", lambda e, pp=pp: e.tensor_copy(
                    att_scr[pp][:, 0:HT], ps2[pp][:, 0:HT]),
                    [mm, att_rel.get(pp)], "attev")
                ps2_rel[pp] = aev
                sd_ = dma("sy", d_asp[m, :, half * HT:(half + 1) * HT],
                          att_scr[pp][:, 0:HT], deps=[aev], tag="aspill",
                          stream=f"as{pp}")
                att_rel[pp] = sd_

        P.barrier("endP2b")

        pf2 = [pf_a, pf_b]
        pf2_rel = {}
        prev_yn = None
        prev_rd = None
        for p in range(NPOS):
            lda = dma("sy", attnb,
                      d_asp.rearrange("m q c -> q m c")[:, :, p * 128:(p + 1) * 128],
                      deps=[prev_rd], tag="attld", stream="attld")
            ldx = dma("sy", xqt, d_xq[p], deps=[prev_rd], tag="xqld", stream="xqld")
            ad = None
            for mc in range(NJ):
                fp = mc % 2
                tr = add("t", lambda e, mc=mc, fp=fp: e.transpose(
                    pf2[fp][:, 0:128], attnb[:, mc * 128:(mc + 1) * 128], eye32),
                    [lda, ld_eye32, pf2_rel.get(fp)], "attr")
                ad = add("v", lambda e, mc=mc, fp=fp: e.tensor_tensor(
                    y_sb[:, mc * 128:(mc + 1) * 128], pf2[fp][:, 0:128],
                    xqt[:, mc * 128:(mc + 1) * 128], ALU.add),
                    [tr, ldx, prev_yn], "yadd")
                pf2_rel[fp] = ad
            ysd = dma("sy", d_ysp[p], y_sb, deps=[ad], tag="yspill", stream="ysp")
            ap9 = ln_chain(p % 2, y_sb, yn_sb, [ad, prev_yn, ysd])
            prev_rd = ad

            def dst_fn(jg, p=p):
                da = BUF[:, :].rearrange("q (j c) -> q j c", c=OWN)[
                    :, jg * 8:(jg + 1) * 8, p * 128:(p + 1) * 128]
                return da, []
            evs = transpose_tile(yn_sb, ap9, dst_fn, tag=f"y{p}")
            prev_yn = evs[-1]

        P.barrier("endP2c")

        psg = [[ps_a[:, 0:512], ps_a[:, 512:1024]],
               [ps_b[:, 0:512], ps_b[:, 512:1024]]]
        win_rel = {}
        wT_rel = {}
        psf_rel = {}
        hi_rel = {}
        silu_rel = {}
        wdt_rel = {}
        st_f1 = {}
        pending = []
        ub = 0

        def flush(nmax=1):
            while len(pending) > nmax:
                pending.pop(0)()

        for i_ffn in range(NI):
            i = i_ffn
            ipar = i % 2
            gu_last = [[None] * NHALF, [None] * NHALF]
            for mi, dw in enumerate((d_wg, d_wu, d_wd)):
                for cg in range(4):
                    buf = ub % 4
                    ub += 1
                    if mi < 2:
                        wsrc = dw[i * 128:(i + 1) * 128,
                                  cg * 1024:(cg + 1) * 1024]
                    else:
                        wsrc = dw.rearrange("(j q) c -> q j c", q=128)[
                            :, cg * 8:(cg + 1) * 8, i * 128:(i + 1) * 128]
                    ldw = dma("gp", wbf_in[buf], wsrc,
                              deps=[win_rel.get(buf)], tag="wld",
                              stream=f"wl{buf}")
                    pbp = buf % 2
                    trs = []
                    for s in range(8):
                        trs.append(add("t", lambda e, s=s, buf=buf, pbp=pbp:
                                       e.transpose(
                            pb_t[pbp][:, s * 128:(s + 1) * 128],
                            wbf_in[buf][:, s * 128:(s + 1) * 128], eye16),
                            [ldw, ld_eye16, pb_rel.get(pbp)], "wtr"))
                    win_rel[buf] = trs[-1]
                    if mi < 2:
                        dst = wT[buf]
                        xd = [wT_rel.get(buf)]
                    else:
                        dst = wdtcg[cg % 2]
                        xd = [wdt_rel.get(cg % 2)]
                    tev = add("v", lambda e, dst=dst, pbp=pbp: e.tensor_copy(
                        dst, pb_t[pbp][:, :]), trs + xd, "wtev")
                    pb_rel[pbp] = tev

                    if mi == 2:
                        wd_dma = dma("sy", d_wdt[i, :, cg * 1024:(cg + 1) * 1024],
                                     wdtcg[cg % 2], deps=[tev],
                                     tag="wdtout", stream=f"wd{cg % 2}")
                        wdt_rel[cg % 2] = wd_dma
                        continue

                    def mk_mm(i=i, mi=mi, cg=cg, buf=buf, tev=tev,
                              gu_last=gu_last):
                        def emit():
                            mm = None
                            for half in range(NHALF):
                                hp = half % 2
                                for s in range(8):
                                    j = cg * 8 + s
                                    dep = [tev]
                                    if j == 0:
                                        dep.append(psf_rel.get((mi, hp)))
                                    mm = add("t", lambda e, s=s, j=j, mi=mi,
                                             buf=buf, half=half, hp=hp:
                                             e.matmul(
                                        psg[mi][hp][:, 0:HT],
                                        wT[buf][:, s * 128:(s + 1) * 128],
                                        BUF[:, j * OWN + half * HT:
                                            j * OWN + (half + 1) * HT],
                                        start=(j == 0), stop=(j == NJ - 1)),
                                        dep, "gumm")
                                gu_last[mi][half] = mm
                            wT_rel[buf] = mm
                        return emit
                    pending.append(mk_mm())
                    flush()

            def mk_epi(i=i, ipar=ipar, gu_last=gu_last):
                def emit():
                    nonlocal_state = None
                    hlast = None
                    for half in range(NHALF):
                        hp = half % 2
                        sev = add("s", lambda e, i=i, hp=hp: e.activation(
                            silu_scr[hp][:, 0:HT], psg[0][hp][:, 0:HT],
                            AF.Silu, scale=sg_sb[:, i:i + 1]),
                            [gu_last[0][half], ld_sg, silu_rel.get(hp)],
                            "silu")
                        hev = add("v", lambda e, i=i, hp=hp, ipar=ipar,
                                  half=half: e.scalar_tensor_tensor(
                            h_i[ipar][:, half * HT:(half + 1) * HT],
                            psg[1][hp][:, 0:HT], su_sb[:, i:i + 1],
                            silu_scr[hp][:, 0:HT], ALU.mult, ALU.mult),
                            [sev, gu_last[1][half], ld_su,
                             hi_rel.get(ipar) if half == 0 else None], "hmul")
                        silu_rel[hp] = hev
                        psf_rel[(0, hp)] = sev
                        psf_rel[(1, hp)] = hev
                        hlast = hev
                    cres = add("v", lambda e, i=i, ipar=ipar: e.tensor_copy(
                        h_all[:, i * GTP:(i + 1) * GTP],
                        h_i[ipar][:, 0:GTP]), [hlast, st_f1.get("hres")],
                        "hres")
                    st_f1["hres"] = cres
                    rel = cres
                    if NPASS > 1:
                        rel = dma("sy", d_hsp[i], h_i[ipar][:, GTP:OWN],
                                  deps=[hlast], tag="hspill",
                                  stream=f"hs{ipar}")
                    hi_rel[ipar] = rel
                return emit
            pending.append(mk_epi())
            flush()
        flush(0)

        P.barrier("endF1")

        psd = [ps_a[:, 0:GTP], ps_b[:, 0:GTP]]
        psd_rel = {}
        strip_rel = {}
        y_rel = {}
        orel = {}
        d_rel = {}
        pf3_rel = {}
        pass_last = None
        NSP = 4
        for ipass in range(NPASS):
            hloads = [None] * NSP
            if ipass > 0:
                for c in range(NSP):
                    i0 = (NI * c) // NSP
                    i1 = (NI * (c + 1)) // NSP
                    hsrc = d_hsp.rearrange("i q c -> q i c")[
                        :, i0:i1, (ipass - 1) * GTP: ipass * GTP]
                    hloads[c] = dma(("sy", "gp")[c % 2],
                                    h_all[:, i0 * GTP:i1 * GTP], hsrc,
                                    deps=[pass_last], tag="hl",
                                    stream="hl" + ("sy", "gp")[c % 2])
            for j in range(NJ):
                sp = j % 2
                lds = []
                for c in range(NSP):
                    i0 = (NI * c) // NSP
                    i1 = (NI * (c + 1)) // NSP
                    lds.append(dma(("sy", "gp")[c % 2],
                                   strip[sp][:, i0 * 128:i1 * 128],
                                   d_wdt.rearrange("i q c -> q i c")[
                                       :, i0:i1, j * 128:(j + 1) * 128],
                                   deps=[strip_rel.get(sp)], tag="strip",
                                   stream=f"st{sp}{c % 2}"))
                gp_ = j % 2
                cmap = {}
                for c in range(NSP):
                    for i in range((NI * c) // NSP, (NI * (c + 1)) // NSP):
                        cmap[i] = c
                mm = None
                for i in range(NI):
                    c = cmap[i]
                    dep = [lds[c], hloads[c]]
                    if i == 0:
                        dep.append(psd_rel.get(gp_))
                    mm = add("t", lambda e, i=i, sp=sp, gp_=gp_: e.matmul(
                        psd[gp_], strip[sp][:, i * 128:(i + 1) * 128],
                        h_all[:, i * GTP:(i + 1) * GTP],
                        start=(i == 0), stop=(i == NI - 1)), dep, "dmm")
                strip_rel[sp] = mm
                pass_last = mm
                dev = add("v", lambda e, j=j, gp_=gp_: e.tensor_scalar_mul(
                    dscr[gp_], psd[gp_], sd_sb[:, j:j + 1]),
                    [mm, ld_sd, d_rel.get(gp_)], "dscl")
                psd_rel[gp_] = dev
                last_tr = None
                for tt in range(GTP // 128):
                    ptile = ipass * (GTP // 128) + tt
                    fp = (j * (GTP // 128) + tt) % 2
                    tr = add("t", lambda e, tt=tt, gp_=gp_, fp=fp:
                             e.transpose(
                        pf2[fp][:, 0:128],
                        dscr[gp_][:, tt * 128:(tt + 1) * 128], eye32),
                        [dev, ld_eye32, pf3_rel.get(fp)], "dtr")
                    last_tr = tr
                    yld = dma("gp", yscr[fp],
                              d_ysp[ptile, :, j * 128:(j + 1) * 128],
                              deps=[y_rel.get(fp)], tag="yld",
                              stream=f"yl{fp}")
                    oev = add("v", lambda e, fp=fp: e.tensor_tensor(
                        oscr[fp], pf2[fp][:, 0:128], yscr[fp], ALU.add),
                        [tr, yld, orel.get(fp)], "oadd")
                    pf3_rel[fp] = oev
                    y_rel[fp] = oev
                    od = dma("gp", d_out[ptile, :, j * 128:(j + 1) * 128],
                             oscr[fp], deps=[oev], tag="outw",
                             stream=f"ow{fp}")
                    orel[fp] = od
                d_rel[gp_] = last_tr

                P.barrier("final")

        run_engine = P.emit_fns(get_sem)
        with nc.Block() as blk:
            blk.sync(lambda e: run_engine(e, "sy"))
            blk.gpsimd(lambda e: run_engine(e, "gp"))
            blk.tensor(lambda e: run_engine(e, "t"))
            blk.vector(lambda e: run_engine(e, "v"))
            blk.scalar(lambda e: run_engine(e, "s"))

    return nc


def _own_tiles(c, NT):
    na = NT // NCORES // 2
    return ([c + 8 * j for j in range(na)]
            + [NT - 1 - c - 8 * j for j in range(na)])


def _masks_for_core(c, NT):
    NPOS = NT // NCORES
    na = NPOS // 2
    m = np.zeros((NPOS, 128, 1024), dtype=np.float32)
    tiles = _own_tiles(c, NT)
    tri = np.where(np.arange(1024)[None, :] % 128 <= np.arange(128)[:, None],
                   0.0, MASK_NEG).astype(np.float32)
    for p, t in enumerate(tiles):
        bound = 8 * (p + 1) if p < na else NT - 8 * (p - na)
        for s in range(8):
            abs_sub = bound - 8 + s
            if abs_sub < t:
                continue
            if abs_sub == t:
                m[p, :, s * 128:(s + 1) * 128] = tri[:, s * 128:(s + 1) * 128]
            else:
                m[p, :, s * 128:(s + 1) * 128] = MASK_NEG
    return m


def make_inputs(x, wq, wk, wv, wo, wg, sgv, wu, suv, wd, sdv, T, I_FFN):
    NT = T // 128
    NI = I_FFN // 128
    base = {
        "x": np.ascontiguousarray(x.reshape(T, D)),
        "wq": np.ascontiguousarray(wq), "wk": np.ascontiguousarray(wk),
        "wv": np.ascontiguousarray(wv), "wo": np.ascontiguousarray(wo),
        "wg": np.ascontiguousarray(wg), "wu": np.ascontiguousarray(wu),
        "wd": np.ascontiguousarray(wd),
        "sg": np.ascontiguousarray(sgv.reshape(NI, 128).T),
        "su": np.ascontiguousarray(suv.reshape(NI, 128).T),
        "sd": np.ascontiguousarray(sdv.reshape(NJ, 128).T),
        "eye16": np.eye(128, dtype=ml_dtypes.bfloat16),
        "eye32": np.eye(128, dtype=np.float32),
    }
    xr = base["x"].reshape(NT, 128, D)
    ins = []
    for c in range(NCORES):
        m = dict(base)
        m["xq"] = np.ascontiguousarray(xr[_own_tiles(c, NT)])
        m["mask"] = _masks_for_core(c, NT)
        ins.append(m)
    return ins


def assemble_output(results, T):
    NT = T // 128
    out = np.zeros((NT, 128, D), dtype=np.float32)
    for c in range(NCORES):
        out[_own_tiles(c, NT)] = results[c]["out"]
    return out.reshape(1, T, D)


def kernel(L0_attn_x, L0_attn_wq, L0_attn_wk, L0_attn_wv, L0_attn_wo,
           L0_ffn_w_gate, L0_ffn_s_gate, L0_ffn_w_up, L0_ffn_s_up,
           L0_ffn_w_down, L0_ffn_s_down):
    T = int(L0_attn_x.shape[1])
    I_FFN = int(L0_ffn_w_gate.shape[0])
    nc = build_nc(T=T, I_FFN=I_FFN)
    ins = make_inputs(np.asarray(L0_attn_x, dtype=np.float32).reshape(T, D),
                      np.asarray(L0_attn_wq, dtype=np.float32),
                      np.asarray(L0_attn_wk, dtype=np.float32),
                      np.asarray(L0_attn_wv, dtype=np.float32),
                      np.asarray(L0_attn_wo, dtype=np.float32),
                      np.asarray(L0_ffn_w_gate, dtype=np.int32),
                      np.asarray(L0_ffn_s_gate, dtype=np.float32),
                      np.asarray(L0_ffn_w_up, dtype=np.int32),
                      np.asarray(L0_ffn_s_up, dtype=np.float32),
                      np.asarray(L0_ffn_w_down, dtype=np.int32),
                      np.asarray(L0_ffn_s_down, dtype=np.float32),
                      T, I_FFN)
    res = run_bass_kernel_spmd(nc, ins, core_ids=list(range(NCORES)))
    return assemble_output(res.results, T)


# version 09
# speedup vs baseline: 1.1761x; 1.0646x over previous
import numpy as np
import ml_dtypes
from contextlib import ExitStack

import concourse.bass as bass
import concourse.mybir as mybir
from concourse.bass_utils import run_bass_kernel_spmd

F32 = mybir.dt.float32
BF16 = mybir.dt.bfloat16
I32 = mybir.dt.int32
U8 = mybir.dt.uint8
AF = mybir.ActivationFunctionType
ALU = mybir.AluOpType
AX = mybir.AxisListType

NCORES = 8
D = 4096
NH = 32
DH = 128
NJ = D // 128
EPS = 1e-5
QSCALE = 1.0 / float(np.sqrt(DH))
MASK_NEG = -30000.0

ENGS = ("sy", "gp", "t", "v", "s")
CHAIN_ENGS = ()
DECLARED_SELF = ("v", "s")


class Prog:

    def __init__(self):
        self.ev = []
        self.last_of = {e: None for e in ENGS}
        self.last_stream = {}

    def add(self, eng, fn, deps=(), tag="", stream=None):
        i = len(self.ev)
        deps = [d for d in deps if d is not None]
        if stream is not None:
            prev = self.last_stream.get((eng, stream))
            if prev is not None:
                deps.append(prev)
            self.last_stream[(eng, stream)] = i
        self.ev.append(dict(i=i, eng=eng, fn=fn, deps=deps, tag=tag,
                            stream=stream, need=False, val=None, sem=None))
        self.last_of[eng] = i
        return i

    def barrier(self, tag="bar"):
        lasts = [self.last_of[e] for e in ENGS if self.last_of[e] is not None]
        lasts += [v for v in self.last_stream.values()]
        out = {}
        for e in ENGS:
            out[e] = self.add(e, None, deps=list(lasts), tag=tag)
        return out

    def emit_fns(self, get_sem):
        ev = self.ev
        for e_ in ev:
            for d in e_["deps"]:
                ev[d]["need"] = True
        counters = {}
        for e_ in ev:
            eng = e_["eng"]
            if eng in ("sy", "gp"):
                if e_["fn"] is not None:
                    sname = f"D:{eng}:{e_['stream'] or 'def'}"
                    counters[sname] = counters.get(sname, 0) + 16
                    e_["sem"], e_["val"] = sname, counters[sname]
                else:
                    sname = eng + "m"
                    counters[sname] = counters.get(sname, 0) + 1
                    e_["sem"], e_["val"] = sname, counters[sname]
            elif eng in CHAIN_ENGS or e_["need"] or e_["fn"] is None:
                counters[eng] = counters.get(eng, 0) + 1
                e_["sem"], e_["val"] = eng, counters[eng]
        per_eng = {e: [] for e in ENGS}
        for e_ in ev:
            per_eng[e_["eng"]].append(e_)

        def run_engine(handle, eng):
            satisfied = {}
            prev_val = None
            for e_ in per_eng[eng]:
                waits = {}
                for d in e_["deps"]:
                    p = ev[d]
                    if p["val"] is None:
                        continue
                    if p["eng"] == eng and not p["sem"].startswith("D:"):
                        if eng not in DECLARED_SELF:
                            continue
                    if (p["eng"] == eng and e_["fn"] is not None
                            and e_["sem"] == p["sem"]):
                        pass
                    sname = p["sem"]
                    if satisfied.get(sname, 0) >= p["val"]:
                        continue
                    waits[sname] = max(waits.get(sname, 0), p["val"])
                if eng in CHAIN_ENGS and prev_val is not None:
                    waits[eng] = max(waits.get(eng, 0), prev_val)
                for se, v in sorted(waits.items()):
                    handle.wait_ge(get_sem(se), v)
                    satisfied[se] = v
                if e_["fn"] is not None:
                    inst = e_["fn"](handle)
                    if e_["val"] is not None:
                        inst.then_inc(get_sem(e_["sem"]),
                                      16 if e_["sem"].startswith("D:") else 1)
                elif e_["val"] is not None:
                    handle.nop().then_inc(get_sem(e_["sem"]), 1)
                if e_["val"] is not None and e_["sem"] == eng:
                    prev_val = e_["val"]
        return run_engine


def build_nc(T=8192, I_FFN=11008):
    NT = T // 128
    NPOS = NT // NCORES
    NA = NPOS // 2
    OWN = NPOS * 128
    NI = I_FFN // 128
    GTP = min(512, OWN)
    NPASS = OWN // GTP
    NHALF = max(1, OWN // 512)
    HT = OWN // NHALF

    def nch(p):
        return (p + 1) if p < NA else (NT // 8 - (p - NA))

    nc = bass.Bass("TRN2", target_bir_lowering=False)

    d_x = nc.dram_tensor("x", [T, D], F32, kind="ExternalInput").ap()
    d_xq = nc.dram_tensor("xq", [NPOS, 128, D], F32, kind="ExternalInput").ap()
    d_wq = nc.dram_tensor("wq", [D, D], F32, kind="ExternalInput").ap()
    d_wk = nc.dram_tensor("wk", [D, DH], F32, kind="ExternalInput").ap()
    d_wv = nc.dram_tensor("wv", [D, DH], F32, kind="ExternalInput").ap()
    d_wo = nc.dram_tensor("wo", [D, D], F32, kind="ExternalInput").ap()
    d_wg = nc.dram_tensor("wg", [I_FFN, D], I32, kind="ExternalInput").ap()
    d_wu = nc.dram_tensor("wu", [I_FFN, D], I32, kind="ExternalInput").ap()
    d_wd = nc.dram_tensor("wd", [D, I_FFN], I32, kind="ExternalInput").ap()
    d_sg = nc.dram_tensor("sg", [128, NI], F32, kind="ExternalInput").ap()
    d_su = nc.dram_tensor("su", [128, NI], F32, kind="ExternalInput").ap()
    d_sd = nc.dram_tensor("sd", [128, NJ], F32, kind="ExternalInput").ap()
    d_eye16 = nc.dram_tensor("eye16", [128, 128], BF16, kind="ExternalInput").ap()
    d_eye32 = nc.dram_tensor("eye32", [128, 128], F32, kind="ExternalInput").ap()
    d_mask = nc.dram_tensor("mask", [NPOS, 128, 1024], F32, kind="ExternalInput").ap()
    d_out = nc.dram_tensor("out", [NPOS, 128, D], F32, kind="ExternalOutput").ap()
    d_wdt = nc.dram_tensor("wdt_s", [NI, 128, D], BF16).ap()
    d_hsp = (nc.dram_tensor("h_sp", [NI, 128, OWN - GTP], BF16).ap()
             if NPASS > 1 else None)
    d_ysp = nc.dram_tensor("y_sp", [NPOS, 128, D], F32).ap()
    d_asp = nc.dram_tensor("attn_sp", [NJ, 128, OWN], F32).ap()

    P = Prog()
    ARENA = 204800

    with ExitStack() as stk:
        en = stk.enter_context
        AR = en(nc.sbuf_tensor("AR", [128, ARENA], U8))
        arf = AR[:, :].bitcast(F32)
        arb = AR[:, :].bitcast(BF16)

        def fbuf(off, cols):
            assert off % 4 == 0 and off + 4 * cols <= ARENA, (off, cols)
            return arf[:, off // 4: off // 4 + cols]

        def bbuf(off, cols):
            assert off % 2 == 0 and off + 2 * cols <= ARENA, (off, cols)
            return arb[:, off // 2: off // 2 + cols]

        eye16 = bbuf(0, 128)
        eye32 = fbuf(512, 128)
        sg_sb = fbuf(1024, NI)
        su_sb = fbuf(1536, NI)
        sd_sb = fbuf(2048, NJ)
        ST = 2560

        def stat(slot):
            return fbuf(ST + 4 * slot, 1)
        wk_sb = bbuf(3072, NJ * 128)
        wv_sb = bbuf(11264, NJ * 128)
        BUFO = 19456
        BUF = bbuf(BUFO, NH * OWN)
        PH = BUFO + NH * OWN * 2
        KT = bbuf(PH, NT * 128)
        VST = 132
        VSB = bbuf(PH + NT * 256, NT * VST)
        PA = PH + NT * 256 + ((NT * VST * 2 + 511) // 512) * 512
        xt = [fbuf(PA + i * 16384, D) for i in range(2)]
        xn = [bbuf(PA + 32768 + i * 8192, D) for i in range(2)]
        xnT = [bbuf(PA + 49152 + i * 8192, D) for i in range(2)]
        wq_sb = [bbuf(PA + 16384 + i * 8192, 8 * 512) for i in range(2)]
        xnqT = bbuf(PA + 49152, NJ * HT)
        mask_sb = fbuf(PA, NPOS * 1024)
        p_sb = [bbuf(PA + 32768 + i * 2048, 1024) for i in range(2)]
        pt_sb = [bbuf(PA + 36864 + i * 2048, 1024) for i in range(2)]
        o_sbs = [bbuf(PA + 40960, D), bbuf(PA + 51712, D)]
        o_acc = [fbuf(PA + 49152 + i * 640, VST) for i in range(4)]
        wo_sb = [bbuf(PH + i * 8192, NJ * 128) for i in range(2)]
        att_scr = [fbuf(PH + 16384 + i * 2048, 512) for i in range(2)]
        attnb = fbuf(PA, D)
        xqt = fbuf(PA + 16384, D)
        y_sb = fbuf(PA + 32768, D)
        yn_sb = bbuf(PA + 49152, D)
        h_all = bbuf(PH, NI * GTP)
        F1B = PH + ((NI * GTP * 2 + 511) // 512) * 512
        wbf_in = [bbuf(F1B + b * 2048, 1024) for b in range(4)]
        wT = [bbuf(F1B + 8192 + b * 2048, 1024) for b in range(4)]
        wdtcg = [bbuf(F1B + 16384 + b * 2048, 1024) for b in range(2)]
        h_i = [bbuf(F1B + 20480 + b * ((OWN * 2 + 511) // 512) * 512, OWN)
               for b in range(2)]
        HIE = F1B + 20480 + 2 * ((OWN * 2 + 511) // 512) * 512
        silu_scr = [fbuf(HIE + b * 2048, 512) for b in range(2)]
        assert HIE + 4096 <= ARENA, (HIE, ARENA)
        strip = [bbuf(BUFO + i * NI * 256, NI * 128) for i in range(2)]
        SS = BUFO + 2 * NI * 256
        SCR = SS if SS + 12288 <= BUFO + NH * OWN * 2 else HIE + 4096
        assert SCR + 12288 <= ARENA
        dscr = [fbuf(SCR + i * 2048, GTP) for i in range(2)]
        yscr = [fbuf(SCR + 4096 + i * 2048, GTP) for i in range(2)]
        oscr = [fbuf(SCR + 8192 + i * 2048, GTP) for i in range(2)]

        ps_a = en(nc.psum_tensor("ps_a", [128, 1024], F32))
        ps_b = en(nc.psum_tensor("ps_b", [128, 1024], F32))
        pf_a = en(nc.psum_tensor("pf_a", [128, 512], F32))
        pf_b = en(nc.psum_tensor("pf_b", [128, 512], F32))
        pb_t = [en(nc.psum_tensor(f"pb{i}", [128, 1024], BF16)) for i in range(2)]
        sem_cache = {}

        def get_sem(name):
            if name not in sem_cache:
                sem_cache[name] = en(nc.semaphore(
                    "sem_" + name.replace(":", "_")))
            return sem_cache[name]

        def dma(q, out_ap, in_ap, deps=(), tag="", stream=None):
            return P.add(q, lambda e, o=out_ap, i=in_ap: e.dma_start(o, i),
                         deps, tag, stream=(stream or tag or "def"))

        add = P.add

        ld_eye16 = dma("sy", eye16, d_eye16, tag="eye16", stream="c")
        ld_eye32 = dma("sy", eye32, d_eye32, tag="eye32", stream="c")
        ld_sg = dma("sy", sg_sb, d_sg, stream="c")
        ld_su = dma("sy", su_sb, d_su, stream="c")
        ld_sd = dma("sy", sd_sb, d_sd, stream="c")
        ld_wk = dma("gp", wk_sb, d_wk.rearrange("(j p) c -> p j c", p=128), stream="gc")
        ld_wv = dma("gp", wv_sb, d_wv.rearrange("(j p) c -> p j c", p=128), stream="gc")
        ld_ones = None
        eps_ap = stat(60)
        ld_eps = add("v", lambda e: e.memset(eps_ap, float(EPS)), [], "eps")

        def ln_chain(par, src, dst_bf, deps_in):
            s_sum, s_sq = stat(0 + par), stat(2 + par)
            s_mean, s_var = stat(4 + par), stat(6 + par)
            s_sd, s_rstd = stat(8 + par), stat(10 + par)
            s_negmr, s_m2 = stat(12 + par), stat(14 + par)
            e1 = add("v", lambda e: e.reduce_sum(s_sum, src, axis=AX.X),
                     deps_in, "ln_sum")
            e2 = add("s", lambda e: e.activation(dst_bf, src, AF.Square,
                                                 accum_out=s_sq),
                     deps_in, "ln_sq")
            e3 = add("v", lambda e: e.tensor_scalar_mul(s_mean, s_sum, 1.0 / D),
                     [e1], "ln_mean")
            e4 = add("v", lambda e: e.tensor_mul(s_m2, s_mean, s_mean), [e3])
            e5 = add("v", lambda e: e.scalar_tensor_tensor(
                s_var, s_sq, 1.0 / D, s_m2, ALU.mult, ALU.subtract), [e2, e4])
            e6 = add("s", lambda e: e.activation(s_sd, s_var, AF.Sqrt,
                                                 bias=eps_ap), [e5, ld_eps])
            e7 = add("v", lambda e: e.reciprocal(s_rstd, s_sd), [e6])
            e8 = add("v", lambda e: e.scalar_tensor_tensor(
                s_negmr, s_mean, -1.0, s_rstd, ALU.mult, ALU.mult), [e7, e3])
            e9 = add("s", lambda e: e.activation(dst_bf, src, AF.Identity,
                                                 bias=s_negmr, scale=s_rstd),
                     [e8], "ln_apply")
            return e9

        pb_rel = {}

        def transpose_tile(src_bf, dep_src, dst_fn, tag=""):
            evs = []
            for jg in range(NJ // 8):
                pbk = pb_t[jg % 2]
                rel = pb_rel.get(jg % 2)
                trs = []
                for s in range(8):
                    j = jg * 8 + s
                    trs.append(add("t", lambda e, j=j, s=s, p=pbk:
                                   e.transpose(p[:, s * 128:(s + 1) * 128],
                                               src_bf[:, j * 128:(j + 1) * 128],
                                               eye16),
                                   [dep_src, ld_eye16, rel], tag + "tr"))
                dst_ap, xdeps = dst_fn(jg)
                ev = add("v", lambda e, p=pbk, da=dst_ap: e.tensor_copy(da, p[:, :]),
                         trs + list(xdeps), tag + "ev")
                pb_rel[jg % 2] = ev
                evs.append(ev)
            return evs

        ld_ones = add("v", lambda e: e.memset(
            VSB[:, :].rearrange("q (t c) -> q t c", c=VST)[:, :, 128:129], 1.0),
            [], "vones")
        xt_rel = {}
        xn_rel = {}
        xnT_rel = {}
        kv_rel = {}
        last_kv = [None, None]
        for i in range(NT):
            par = i % 2
            ld = dma("sy", xt[par], d_x[i * 128:(i + 1) * 128, :],
                     deps=[xt_rel.get(par)], tag=f"x{i}", stream=f"x{par}")
            ap9 = ln_chain(par, xt[par], xn[par], [ld, xn_rel.get(par)])
            xt_rel[par] = ap9

            def dst_fn(jg, par=par):
                return xnT[par][:, jg * 1024:(jg + 1) * 1024], \
                    ([xnT_rel.get(par)] if xnT_rel.get(par) is not None else [])
            evacs = transpose_tile(xn[par], ap9, dst_fn, tag=f"a{i}")
            xn_rel[par] = evacs[-1]
            kmm = vmm = None
            for j in range(NJ):
                dep = [evacs[j // 8], ld_wk]
                if j == 0:
                    dep.append(kv_rel.get("k"))
                kmm = add("t", lambda e, j=j, par=par: e.matmul(
                    pf_a[:, 0:128], wk_sb[:, j * 128:(j + 1) * 128],
                    xnT[par][:, j * 128:(j + 1) * 128],
                    start=(j == 0), stop=(j == NJ - 1)), dep, "kmm")
            kev = add("v", lambda e, i=i: e.tensor_copy(
                KT[:, i * 128:(i + 1) * 128], pf_a[:, 0:128]), [kmm], "kev")
            kv_rel["k"] = kev
            for j in range(NJ):
                dep = [evacs[j // 8], ld_wv]
                if j == 0:
                    dep.append(kv_rel.get("v"))
                vmm = add("t", lambda e, j=j, par=par: e.matmul(
                    pf_b[:, 0:128], xnT[par][:, j * 128:(j + 1) * 128],
                    wv_sb[:, j * 128:(j + 1) * 128],
                    start=(j == 0), stop=(j == NJ - 1)), dep, "vmm")
            vev = add("v", lambda e, i=i: e.tensor_copy(
                VSB[:, i * VST:i * VST + 128], pf_b[:, 0:128]), [vmm], "vev")
            kv_rel["v"] = vev
            xnT_rel[par] = vmm
            last_kv = [kev, vev]

        P.barrier("endP1a")

        qt_last = None
        psq = [ps_a[:, 0:HT], ps_a[:, 512:512 + HT],
               ps_b[:, 0:HT], ps_b[:, 512:512 + HT]]
        psq_rel = [None] * 4
        wq_rel = {}
        xq_rel = None
        for half in range(NHALF):
            ev_groups = {jg: [] for jg in range(4)}
            prev_ln = None
            for ti in range(HT // 128):
                p = half * (HT // 128) + ti
                ld = dma("sy", xt[0], d_xq[p], deps=[xq_rel], tag=f"xq{p}", stream="xq")
                ap9 = ln_chain(p % 2, xt[0], xn[p % 2], [ld, prev_ln])
                xq_rel = ap9

                def dst_fn(jg, ti=ti, half=half):
                    da = xnqT[:, :].rearrange("q (j c) -> q j c", c=HT)[
                        :, jg * 8:(jg + 1) * 8, ti * 128:(ti + 1) * 128]
                    xd = [qt_last] if (half > 0 and ti == 0) else []
                    return da, xd
                evacs = transpose_tile(xn[p % 2], ap9, dst_fn, tag=f"b{p}")
                prev_ln = evacs[-1]
                for jg in range(4):
                    ev_groups[jg].append(evacs[jg])
            for mb in range(8):
                last_mm = None
                for jg in range(4):
                    wpar = (mb * 4 + jg) % 2
                    src = d_wq.rearrange("(j p) c -> p j c", p=128)[
                        :, jg * 8:(jg + 1) * 8, mb * 512:(mb + 1) * 512]
                    ldw = dma("gp", wq_sb[wpar], src,
                              deps=[wq_rel.get(wpar)], tag="wq",
                              stream=f"wq{wpar}")
                    for jj in range(8):
                        j = jg * 8 + jj
                        for m in range(4):
                            dep = [ldw] + ev_groups[jg]
                            if j == 0:
                                dep.append(psq_rel[m])
                            st_ = (j == 0)
                            sp_ = (j == NJ - 1)
                            last_mm = add("t", lambda e, jj=jj, m=m, wpar=wpar,
                                          j=j, st_=st_, sp_=sp_: e.matmul(
                                psq[m],
                                wq_sb[wpar][:, jj * 512 + m * 128:
                                            jj * 512 + (m + 1) * 128],
                                xnqT[:, j * HT: (j + 1) * HT],
                                start=st_, stop=sp_), dep, "qmm")
                    wq_rel[wpar] = last_mm
                for m in range(4):
                    h = mb * 4 + m
                    ev = add("v", lambda e, h=h, m=m, half=half: e.tensor_scalar_mul(
                        BUF[:, h * OWN + half * HT: h * OWN + (half + 1) * HT],
                        psq[m], QSCALE), [last_mm], "qev")
                    psq_rel[m] = ev
                    qt_last = ev

        P.barrier("endP2b_pre")

        ld_mask = dma("sy", mask_sb, d_mask.rearrange("p q c -> q p c"),
                      tag="mask", stream="mask")
        ps_s = [ps_a, ps_b]
        o_ps = [pf_a, pf_b]
        NPH = 4
        mneg = [stat(16 + i) for i in range(NPH)]
        mpos = [stat(20 + i) for i in range(NPH)]
        alpha = [stat(24 + i) for i in range(4)]
        recip = [stat(36 + i) for i in range(NPH)]
        mtmp = [stat(40 + i) for i in range(NPH)]
        s_rel = {}
        p_rel = {}
        pts_rel = {}
        avps_rel = {}
        ofin_rel = {}
        eev_last = {}
        mneg_last = {}
        alpha_rel = {}
        st_p2a = {}

        recs = []
        hidx = 0
        pend_ot = []
        for p in range(NPOS):
            NCH = nch(p)
            for h in range(NH):
                for n in range(NCH):
                    recs.append(dict(kind="c", p=p, h=h, n=n, nch=NCH,
                                     ph=hidx % NPH))
                hidx += 1
            for jg in range(4):
                pend_ot.append((len(recs) - 1, p, jg))
        out_recs = []
        qi = 0
        src_pos = {}
        for i, r in enumerate(recs):
            out_recs.append(r)
            src_pos[i] = len(out_recs) - 1
            while (qi < len(pend_ot) and pend_ot[qi][0] <= i
                   and len(out_recs) >= src_pos[pend_ot[qi][0]] + 5):
                _, p_, jg_ = pend_ot[qi]
                out_recs.append(dict(kind="ot", p=p_, jg=jg_))
                qi += 1
        while qi < len(pend_ot):
            while len(out_recs) < src_pos[pend_ot[qi][0]] + 4:
                out_recs.append(dict(kind="pad"))
            _, p_, jg_ = pend_ot[qi]
            out_recs.append(dict(kind="ot", p=p_, jg=jg_))
            qi += 1
        recs = out_recs
        for k, r in enumerate(recs):
            r["cp"] = k % 2
            r["k"] = k
        st_p2a["fin"] = {}

        def beatA(r):
            if r["kind"] != "c":
                return
            S = ps_s[r["cp"]]
            p, h, n = r["p"], r["h"], r["n"]
            smm = []
            for sh in range(2):
                dep = [qt_last, last_kv[0]]
                if sh == 0:
                    dep += s_rel.get(r["cp"], [])
                smm.append(add("t", lambda e, p=p, h=h, n=n, sh=sh, S=S:
                               e.matmul(
                    S[:, sh * 512:(sh + 1) * 512],
                    BUF[:, h * OWN + p * 128: h * OWN + (p + 1) * 128],
                    KT[:, n * 1024 + sh * 512: n * 1024 + (sh + 1) * 512],
                    start=True, stop=True), dep, "smm"))
            r["ms"] = smm

        def beatB(r):
            if r["kind"] != "c":
                return
            S = ps_s[r["cp"]]
            ph, n, cp = r["ph"], r["n"], r["cp"]
            p = r["p"]
            ms_after = r["ms"]
            if n == r["nch"] - 1:
                mev = add("v", lambda e, p=p, S=S: e.tensor_tensor(
                    S[:, :], S[:, :],
                    mask_sb[:, p * 1024:(p + 1) * 1024], ALU.add),
                    ms_after + [ld_mask], "mask")
                ms_after = [mev]
            if n == 0:
                rmx = add("v", lambda e, S=S, ph=ph: e.reduce_max(
                    mneg[ph], S[:, :], axis=AX.X, negate=True),
                    ms_after + [eev_last.get(ph), mneg_last.get(ph)], "rmax")
                bias_ready = rmx
                r["aev"] = None
                add("s", lambda e, ph=ph: e.mul(
                    mpos[ph], mneg[ph], -1.0), [rmx], "mpos")
            else:
                rmx = add("v", lambda e, S=S, ph=ph: e.reduce_max(
                    mtmp[ph], S[:, :], axis=AX.X, negate=True),
                    ms_after + [mneg_last.get(ph)], "rmax")
                bias_ready = add("v", lambda e, ph=ph: e.tensor_tensor(
                    mneg[ph], mneg[ph], mtmp[ph], ALU.min),
                    [rmx, eev_last.get(ph), mneg_last.get(ph)], "mmin")
                asl = r["k"] % 4
                r["aev"] = add("s", lambda e, asl=asl, ph=ph: e.activation(
                    alpha[asl], mpos[ph], AF.Exp, bias=mneg[ph]),
                    [bias_ready, alpha_rel.get(asl)], "alpha")
                add("s", lambda e, ph=ph: e.mul(
                    mpos[ph], mneg[ph], -1.0), [r["aev"]], "mpos")
            mneg_last[ph] = bias_ready
            eev = add("s", lambda e, S=S, cp=cp, ph=ph: e.activation(
                p_sb[cp], S[:, :], AF.Exp, bias=mneg[ph]),
                ms_after + [bias_ready, p_rel.get(cp)], "exp")
            eev_last[ph] = eev
            s_rel[cp] = [eev, rmx]
            r["eev"] = eev

        def beatB2(r):
            if r["kind"] == "pad":
                return
            cp = r["cp"]
            if r["kind"] == "ot":
                p, jg = r["p"], r["jg"]
                fin = st_p2a["fin"][p]
                trs = []
                for s in range(8):
                    j = jg * 8 + s
                    trs.append(add("t", lambda e, j=j, s=s, cp=cp, pp=p % 2:
                                   e.transpose(
                        pb_t[cp][:, s * 128:(s + 1) * 128],
                        o_sbs[pp][:, j * 128:(j + 1) * 128], eye16),
                        [fin, ld_eye16, pb_rel.get(cp)], "ottr"))
                r["trs"] = trs
                return
            trs = []
            for s in range(8):
                trs.append(add("t", lambda e, s=s, cp=cp: e.transpose(
                    pb_t[cp][:, s * 128:(s + 1) * 128],
                    p_sb[cp][:, s * 128:(s + 1) * 128], eye16),
                    [r["eev"], ld_eye16, pb_rel.get(cp)], "ptr"))
            p_rel[cp] = trs[-1]
            r["trs"] = trs

        def beatC(r):
            if r["kind"] == "pad":
                return
            cp = r["cp"]
            if r["kind"] == "ot":
                p, jg = r["p"], r["jg"]
                da = BUF[:, :].rearrange("q (j c) -> q j c", c=OWN)[
                    :, jg * 8:(jg + 1) * 8, p * 128:(p + 1) * 128]
                ev = add("v", lambda e, cp=cp, da=da: e.tensor_copy(
                    da, pb_t[cp][:, :]), r["trs"], "otev")
                pb_rel[cp] = ev
                st_p2a[("osb", p % 2)] = ev
                return
            peng = "v" if (r["k"] % 2 == 0) else "s"
            if peng == "v":
                ptev = add("v", lambda e, cp=cp: e.tensor_copy(
                    pt_sb[cp], pb_t[cp][:, :]),
                    r["trs"] + [pts_rel.get(cp)], "ptev")
            else:
                ptev = add("s", lambda e, cp=cp: e.copy(
                    pt_sb[cp], pb_t[cp][:, :]),
                    r["trs"] + [pts_rel.get(cp)], "ptev")
            pb_rel[cp] = ptev
            r["ptev"] = ptev

        def beatD(r):
            if r["kind"] != "c":
                return
            cp, ph, n, p, h = r["cp"], r["ph"], r["n"], r["p"], r["h"]
            avs = None
            for s in range(8):
                a = n * 8 + s
                dep = [r["ptev"], last_kv[1], ld_ones]
                if s == 0:
                    dep.append(avps_rel.get(cp))
                avs = add("t", lambda e, s=s, a=a, cp=cp,
                          st_=(s == 0), sp_=(s == 7): e.matmul(
                    o_ps[cp][:, 0:129],
                    pt_sb[cp][:, s * 128:(s + 1) * 128],
                    VSB[:, a * VST: a * VST + 129],
                    start=st_, stop=sp_), dep, "av")
            pts_rel[cp] = avs
            if n == 0:
                cmb = add("v", lambda e, cp=cp, ph=ph: e.tensor_copy(
                    o_acc[ph][:, 0:129], o_ps[cp][:, 0:129]),
                    [avs, ofin_rel.get(ph)], "ocmb0")
            else:
                asl = r["k"] % 4
                cmb = add("v", lambda e, cp=cp, ph=ph, asl=asl:
                          e.scalar_tensor_tensor(
                    o_acc[ph][:, 0:129], o_acc[ph][:, 0:129],
                    alpha[asl], o_ps[cp][:, 0:129],
                    ALU.mult, ALU.add),
                    [avs, r["aev"], avps_rel.get(("acc", ph))], "ocmb")
                alpha_rel[asl] = cmb
            avps_rel[cp] = cmb
            avps_rel[("acc", ph)] = cmb
            if n == r["nch"] - 1:
                rev = add("v", lambda e, ph=ph: e.reciprocal(
                    recip[ph], o_acc[ph][:, 128:129]), [cmb], "recip")
                fin = add("v", lambda e, h=h, ph=ph, pp=p % 2:
                          e.tensor_scalar_mul(
                    o_sbs[pp][:, h * 128:(h + 1) * 128], o_acc[ph][:, 0:128],
                    recip[ph]), [rev, cmb, st_p2a.get(("osb", p % 2))], "ofin")
                ofin_rel[ph] = fin
                if h == NH - 1:
                    st_p2a["fin"][p] = fin

        NR = len(recs)
        for it in range(NR + 4):
            if it < NR:
                beatA(recs[it])
            if 1 <= it < NR + 1:
                beatB(recs[it - 1])
            if 2 <= it < NR + 2:
                beatB2(recs[it - 2])
            if 3 <= it < NR + 3:
                beatC(recs[it - 3])
            if 4 <= it < NR + 4:
                beatD(recs[it - 4])

        P.barrier("endP2a")

        wo_rel = {}
        att_rel = {}
        ps2_rel = {}
        ps2 = [ps_a[:, 0:512], ps_b[:, 0:512]]
        for m in range(NJ):
            wpar = m % 2
            src = d_wo.rearrange("(j p) c -> p j c", p=128)[
                :, :, m * 128:(m + 1) * 128]
            ldw = dma("gp", wo_sb[wpar], src, deps=[wo_rel.get(wpar)], tag="wo", stream=f"wo{wpar}")
            for half in range(NHALF):
                pp = (m * NHALF + half) % 2
                mm = None
                for j in range(NJ):
                    dep = [ldw]
                    if j == 0:
                        dep.append(ps2_rel.get(pp))
                    mm = add("t", lambda e, j=j, m=m, half=half, wpar=wpar, pp=pp:
                             e.matmul(
                        ps2[pp][:, 0:HT],
                        wo_sb[wpar][:, j * 128:(j + 1) * 128],
                        BUF[:, j * OWN + half * HT: j * OWN + (half + 1) * HT],
                        start=(j == 0), stop=(j == NJ - 1)), dep, "womm")
                wo_rel[wpar] = mm
                aev = add("v", lambda e, pp=pp: e.tensor_copy(
                    att_scr[pp][:, 0:HT], ps2[pp][:, 0:HT]),
                    [mm, att_rel.get(pp)], "attev")
                ps2_rel[pp] = aev
                sd_ = dma("sy", d_asp[m, :, half * HT:(half + 1) * HT],
                          att_scr[pp][:, 0:HT], deps=[aev], tag="aspill",
                          stream=f"as{pp}")
                att_rel[pp] = sd_

        P.barrier("endP2b")

        pf2 = [pf_a, pf_b]
        pf2_rel = {}
        prev_yn = None
        prev_rd = None
        for p in range(NPOS):
            lda = dma("sy", attnb,
                      d_asp.rearrange("m q c -> q m c")[:, :, p * 128:(p + 1) * 128],
                      deps=[prev_rd], tag="attld", stream="attld")
            ldx = dma("sy", xqt, d_xq[p], deps=[prev_rd], tag="xqld", stream="xqld")
            ad = None
            for mc in range(NJ):
                fp = mc % 2
                tr = add("t", lambda e, mc=mc, fp=fp: e.transpose(
                    pf2[fp][:, 0:128], attnb[:, mc * 128:(mc + 1) * 128], eye32),
                    [lda, ld_eye32, pf2_rel.get(fp)], "attr")
                ad = add("v", lambda e, mc=mc, fp=fp: e.tensor_tensor(
                    y_sb[:, mc * 128:(mc + 1) * 128], pf2[fp][:, 0:128],
                    xqt[:, mc * 128:(mc + 1) * 128], ALU.add),
                    [tr, ldx, prev_yn], "yadd")
                pf2_rel[fp] = ad
            ysd = dma("sy", d_ysp[p], y_sb, deps=[ad], tag="yspill", stream="ysp")
            ap9 = ln_chain(p % 2, y_sb, yn_sb, [ad, prev_yn, ysd])
            prev_rd = ad

            def dst_fn(jg, p=p):
                da = BUF[:, :].rearrange("q (j c) -> q j c", c=OWN)[
                    :, jg * 8:(jg + 1) * 8, p * 128:(p + 1) * 128]
                return da, []
            evs = transpose_tile(yn_sb, ap9, dst_fn, tag=f"y{p}")
            prev_yn = evs[-1]

        P.barrier("endP2c")

        psg = [[ps_a[:, 0:512], ps_a[:, 512:1024]],
               [ps_b[:, 0:512], ps_b[:, 512:1024]]]
        win_rel = {}
        wT_rel = {}
        psf_rel = {}
        hi_rel = {}
        silu_rel = {}
        wdt_rel = {}
        st_f1 = {}
        pending = []
        ub = 0

        def flush(nmax=1):
            while len(pending) > nmax:
                pending.pop(0)()

        for i_ffn in range(NI):
            i = i_ffn
            ipar = i % 2
            gu_last = [[None] * NHALF, [None] * NHALF]
            for mi, dw in enumerate((d_wg, d_wu, d_wd)):
                for cg in range(4):
                    buf = ub % 4
                    ub += 1
                    if mi < 2:
                        wsrc = dw[i * 128:(i + 1) * 128,
                                  cg * 1024:(cg + 1) * 1024]
                    else:
                        wsrc = dw.rearrange("(j q) c -> q j c", q=128)[
                            :, cg * 8:(cg + 1) * 8, i * 128:(i + 1) * 128]
                    ldw = dma("gp", wbf_in[buf], wsrc,
                              deps=[win_rel.get(buf)], tag="wld",
                              stream=f"wl{buf}")
                    pbp = buf % 2
                    trs = []
                    for s in range(8):
                        trs.append(add("t", lambda e, s=s, buf=buf, pbp=pbp:
                                       e.transpose(
                            pb_t[pbp][:, s * 128:(s + 1) * 128],
                            wbf_in[buf][:, s * 128:(s + 1) * 128], eye16),
                            [ldw, ld_eye16, pb_rel.get(pbp)], "wtr"))
                    win_rel[buf] = trs[-1]
                    if mi < 2:
                        dst = wT[buf]
                        xd = [wT_rel.get(buf)]
                    else:
                        dst = wdtcg[cg % 2]
                        xd = [wdt_rel.get(cg % 2)]
                    tev = add("v", lambda e, dst=dst, pbp=pbp: e.tensor_copy(
                        dst, pb_t[pbp][:, :]), trs + xd, "wtev")
                    pb_rel[pbp] = tev

                    if mi == 2:
                        wd_dma = dma("sy", d_wdt[i, :, cg * 1024:(cg + 1) * 1024],
                                     wdtcg[cg % 2], deps=[tev],
                                     tag="wdtout", stream=f"wd{cg % 2}")
                        wdt_rel[cg % 2] = wd_dma
                        continue

                    def mk_mm(i=i, mi=mi, cg=cg, buf=buf, tev=tev,
                              gu_last=gu_last):
                        def emit():
                            mm = None
                            for half in range(NHALF):
                                hp = half % 2
                                for s in range(8):
                                    j = cg * 8 + s
                                    dep = [tev]
                                    if j == 0:
                                        dep.append(psf_rel.get((mi, hp)))
                                    mm = add("t", lambda e, s=s, j=j, mi=mi,
                                             buf=buf, half=half, hp=hp:
                                             e.matmul(
                                        psg[mi][hp][:, 0:HT],
                                        wT[buf][:, s * 128:(s + 1) * 128],
                                        BUF[:, j * OWN + half * HT:
                                            j * OWN + (half + 1) * HT],
                                        start=(j == 0), stop=(j == NJ - 1)),
                                        dep, "gumm")
                                gu_last[mi][half] = mm
                            wT_rel[buf] = mm
                        return emit
                    pending.append(mk_mm())
                    flush()

            def mk_epi(i=i, ipar=ipar, gu_last=gu_last):
                def emit():
                    nonlocal_state = None
                    hlast = None
                    for half in range(NHALF):
                        hp = half % 2
                        sev = add("s", lambda e, i=i, hp=hp: e.activation(
                            silu_scr[hp][:, 0:HT], psg[0][hp][:, 0:HT],
                            AF.Silu, scale=sg_sb[:, i:i + 1]),
                            [gu_last[0][half], ld_sg, silu_rel.get(hp)],
                            "silu")
                        hev = add("v", lambda e, i=i, hp=hp, ipar=ipar,
                                  half=half: e.scalar_tensor_tensor(
                            h_i[ipar][:, half * HT:(half + 1) * HT],
                            psg[1][hp][:, 0:HT], su_sb[:, i:i + 1],
                            silu_scr[hp][:, 0:HT], ALU.mult, ALU.mult),
                            [sev, gu_last[1][half], ld_su,
                             hi_rel.get(ipar) if half == 0 else None], "hmul")
                        silu_rel[hp] = hev
                        psf_rel[(0, hp)] = sev
                        psf_rel[(1, hp)] = hev
                        hlast = hev
                    cres = add("v", lambda e, i=i, ipar=ipar: e.tensor_copy(
                        h_all[:, i * GTP:(i + 1) * GTP],
                        h_i[ipar][:, 0:GTP]), [hlast, st_f1.get("hres")],
                        "hres")
                    st_f1["hres"] = cres
                    rel = cres
                    if NPASS > 1:
                        rel = dma("sy", d_hsp[i], h_i[ipar][:, GTP:OWN],
                                  deps=[hlast], tag="hspill",
                                  stream=f"hs{ipar}")
                    hi_rel[ipar] = rel
                return emit
            pending.append(mk_epi())
            flush()
        flush(0)

        P.barrier("endF1")

        psd = [ps_a[:, 0:GTP], ps_b[:, 0:GTP]]
        psd_rel = {}
        strip_rel = {}
        y_rel = {}
        orel = {}
        d_rel = {}
        pf3_rel = {}
        pass_last = None
        NSP = 4
        for ipass in range(NPASS):
            hloads = [None] * NSP
            if ipass > 0:
                for c in range(NSP):
                    i0 = (NI * c) // NSP
                    i1 = (NI * (c + 1)) // NSP
                    hsrc = d_hsp.rearrange("i q c -> q i c")[
                        :, i0:i1, (ipass - 1) * GTP: ipass * GTP]
                    hloads[c] = dma(("sy", "gp")[c % 2],
                                    h_all[:, i0 * GTP:i1 * GTP], hsrc,
                                    deps=[pass_last], tag="hl",
                                    stream="hl" + ("sy", "gp")[c % 2])
            for j in range(NJ):
                sp = j % 2
                lds = []
                for c in range(NSP):
                    i0 = (NI * c) // NSP
                    i1 = (NI * (c + 1)) // NSP
                    lds.append(dma("sy",
                                   strip[sp][:, i0 * 128:i1 * 128],
                                   d_wdt.rearrange("i q c -> q i c")[
                                       :, i0:i1, j * 128:(j + 1) * 128],
                                   deps=[strip_rel.get(sp)], tag="strip",
                                   stream=f"st{sp}{c}"))
                gp_ = j % 2
                cmap = {}
                for c in range(NSP):
                    for i in range((NI * c) // NSP, (NI * (c + 1)) // NSP):
                        cmap[i] = c
                mm = None
                for i in range(NI):
                    c = cmap[i]
                    dep = [lds[c], hloads[c]]
                    if i == 0:
                        dep.append(psd_rel.get(gp_))
                    mm = add("t", lambda e, i=i, sp=sp, gp_=gp_: e.matmul(
                        psd[gp_], strip[sp][:, i * 128:(i + 1) * 128],
                        h_all[:, i * GTP:(i + 1) * GTP],
                        start=(i == 0), stop=(i == NI - 1)), dep, "dmm")
                strip_rel[sp] = mm
                pass_last = mm
                dev = add("v", lambda e, j=j, gp_=gp_: e.tensor_scalar_mul(
                    dscr[gp_], psd[gp_], sd_sb[:, j:j + 1]),
                    [mm, ld_sd, d_rel.get(gp_)], "dscl")
                psd_rel[gp_] = dev
                NTT = GTP // 128
                jb = j % 2
                ysrc = d_ysp.rearrange("p q c -> q p c")[
                    :, ipass * NTT:(ipass + 1) * NTT, j * 128:(j + 1) * 128]
                yld = dma("gp", yscr[jb], ysrc, deps=[y_rel.get(jb)],
                          tag="yld", stream=f"yl{jb}")
                last_tr = None
                oevs = []
                for tt in range(NTT):
                    fp = (j * NTT + tt) % 2
                    tr = add("t", lambda e, tt=tt, gp_=gp_, fp=fp:
                             e.transpose(
                        pf2[fp][:, 0:128],
                        dscr[gp_][:, tt * 128:(tt + 1) * 128], eye32),
                        [dev, ld_eye32, pf3_rel.get(fp)], "dtr")
                    last_tr = tr
                    oev = add("v", lambda e, tt=tt, fp=fp, jb=jb:
                              e.tensor_tensor(
                        oscr[jb][:, tt * 128:(tt + 1) * 128],
                        pf2[fp][:, 0:128],
                        yscr[jb][:, tt * 128:(tt + 1) * 128], ALU.add),
                        [tr, yld, orel.get(jb)], "oadd")
                    pf3_rel[fp] = oev
                    oevs.append(oev)
                y_rel[jb] = oevs[-1]
                odst = d_out.rearrange("p q c -> q p c")[
                    :, ipass * NTT:(ipass + 1) * NTT, j * 128:(j + 1) * 128]
                od = dma("gp", odst, oscr[jb], deps=oevs, tag="outw",
                         stream=f"ow{jb}")
                orel[jb] = od
                d_rel[gp_] = last_tr

                P.barrier("final")

        run_engine = P.emit_fns(get_sem)
        with nc.Block() as blk:
            blk.sync(lambda e: run_engine(e, "sy"))
            blk.gpsimd(lambda e: run_engine(e, "gp"))
            blk.tensor(lambda e: run_engine(e, "t"))
            blk.vector(lambda e: run_engine(e, "v"))
            blk.scalar(lambda e: run_engine(e, "s"))

    return nc


def _own_tiles(c, NT):
    na = NT // NCORES // 2
    return ([c + 8 * j for j in range(na)]
            + [NT - 1 - c - 8 * j for j in range(na)])


def _masks_for_core(c, NT):
    NPOS = NT // NCORES
    na = NPOS // 2
    m = np.zeros((NPOS, 128, 1024), dtype=np.float32)
    tiles = _own_tiles(c, NT)
    tri = np.where(np.arange(1024)[None, :] % 128 <= np.arange(128)[:, None],
                   0.0, MASK_NEG).astype(np.float32)
    for p, t in enumerate(tiles):
        bound = 8 * (p + 1) if p < na else NT - 8 * (p - na)
        for s in range(8):
            abs_sub = bound - 8 + s
            if abs_sub < t:
                continue
            if abs_sub == t:
                m[p, :, s * 128:(s + 1) * 128] = tri[:, s * 128:(s + 1) * 128]
            else:
                m[p, :, s * 128:(s + 1) * 128] = MASK_NEG
    return m


def make_inputs(x, wq, wk, wv, wo, wg, sgv, wu, suv, wd, sdv, T, I_FFN):
    NT = T // 128
    NI = I_FFN // 128
    base = {
        "x": np.ascontiguousarray(x.reshape(T, D)),
        "wq": np.ascontiguousarray(wq), "wk": np.ascontiguousarray(wk),
        "wv": np.ascontiguousarray(wv), "wo": np.ascontiguousarray(wo),
        "wg": np.ascontiguousarray(wg), "wu": np.ascontiguousarray(wu),
        "wd": np.ascontiguousarray(wd),
        "sg": np.ascontiguousarray(sgv.reshape(NI, 128).T),
        "su": np.ascontiguousarray(suv.reshape(NI, 128).T),
        "sd": np.ascontiguousarray(sdv.reshape(NJ, 128).T),
        "eye16": np.eye(128, dtype=ml_dtypes.bfloat16),
        "eye32": np.eye(128, dtype=np.float32),
    }
    xr = base["x"].reshape(NT, 128, D)
    ins = []
    for c in range(NCORES):
        m = dict(base)
        m["xq"] = np.ascontiguousarray(xr[_own_tiles(c, NT)])
        m["mask"] = _masks_for_core(c, NT)
        ins.append(m)
    return ins


def assemble_output(results, T):
    NT = T // 128
    out = np.zeros((NT, 128, D), dtype=np.float32)
    for c in range(NCORES):
        out[_own_tiles(c, NT)] = results[c]["out"]
    return out.reshape(1, T, D)


def kernel(L0_attn_x, L0_attn_wq, L0_attn_wk, L0_attn_wv, L0_attn_wo,
           L0_ffn_w_gate, L0_ffn_s_gate, L0_ffn_w_up, L0_ffn_s_up,
           L0_ffn_w_down, L0_ffn_s_down):
    T = int(L0_attn_x.shape[1])
    I_FFN = int(L0_ffn_w_gate.shape[0])
    nc = build_nc(T=T, I_FFN=I_FFN)
    ins = make_inputs(np.asarray(L0_attn_x, dtype=np.float32).reshape(T, D),
                      np.asarray(L0_attn_wq, dtype=np.float32),
                      np.asarray(L0_attn_wk, dtype=np.float32),
                      np.asarray(L0_attn_wv, dtype=np.float32),
                      np.asarray(L0_attn_wo, dtype=np.float32),
                      np.asarray(L0_ffn_w_gate, dtype=np.int32),
                      np.asarray(L0_ffn_s_gate, dtype=np.float32),
                      np.asarray(L0_ffn_w_up, dtype=np.int32),
                      np.asarray(L0_ffn_s_up, dtype=np.float32),
                      np.asarray(L0_ffn_w_down, dtype=np.int32),
                      np.asarray(L0_ffn_s_down, dtype=np.float32),
                      T, I_FFN)
    res = run_bass_kernel_spmd(nc, ins, core_ids=list(range(NCORES)))
    return assemble_output(res.results, T)
